# version 04
# baseline (speedup 1.0000x reference)
import numpy as np
import concourse.bass as bass
import concourse.mybir as mybir
from concourse.bass_core_rust import IndirectOffsetOnAxis

F32 = mybir.dt.float32
BF16 = mybir.dt.bfloat16
I32 = mybir.dt.int32
U32 = mybir.dt.uint32
AF = mybir.ActivationFunctionType
ALU = mybir.AluOpType
AX = mybir.AxisListType

ENGINES = ("sync", "gpsimd", "tensor", "vector", "scalar")


class Sem:
    def __init__(self, handle):
        self.h = handle
        self.n = 0

    @property
    def now(self):
        return (self, self.n)


class Prog:
    def __init__(self, nc, ctx):
        self.nc = nc
        self.ctx = ctx
        self.q = {e: [] for e in ENGINES}
        self._semid = 0

    def sem(self, name=None):
        self._semid += 1
        h = self.ctx.enter_context(self.nc.semaphore(name or f"s{self._semid}"))
        return Sem(h)

    def on(self, eng, fn):
        self.q[eng].append(fn)

    def wait(self, eng, sem, val=None):
        v = sem.n if val is None else val
        if v > 0:
            self.q[eng].append(lambda e, h=sem.h, v=v: e.wait_ge(h, v))

    def waitv(self, eng, sv):
        sem, v = sv
        self.wait(eng, sem, v)

    def waitt(self, eng, fa):
        if fa is not None:
            self.wait(eng, fa[0], fa[1])

    def inc(self, eng, fn, sem, k=1):
        sem.n += k
        self.q[eng].append(lambda e, f=fn, h=sem.h, k=k: f(e).then_inc(h, k))

    def sinc(self, eng, fn, sem, k=1):
        self.wait(eng, sem, sem.n)
        self.inc(eng, fn, sem, k)

    def dma(self, eng, out, in_, sem, **kw):
        self.inc(eng, lambda e: e.dma_start(out=out, in_=in_, **kw), sem, 16)

    def emit(self):
        blk = self.ctx.enter_context(self.nc.Block())
        for eng in ENGINES:
            ops = self.q[eng]
            if not ops:
                continue
            def body(e, ops=ops):
                for f in ops:
                    f(e)
            getattr(blk, eng)(body)


class FM:
    def __init__(self, t, T, nchunk, nfeat=None):
        self.t = t
        self.T = T
        self.ncks = nchunk
        self.nfeat = nfeat if nfeat is not None else 128 * nchunk

    def ch(self, k, t0=None, t1=None):
        base = k * self.T
        if t0 is None:
            return self.t[:, base:base + self.T]
        return self.t[:, base + t0:base + t1]

    def chp(self, k, p0, p1, t0, t1):
        return self.t[p0:p1, k * self.T + t0:k * self.T + t1]


class Bufs:
    def __init__(self, nc, ctx):
        self.nc = nc
        self.ctx = ctx
        self.d = {}

    def sb(self, name, shape, dt=F32):
        t = self.ctx.enter_context(self.nc.sbuf_tensor(name, list(shape), dt))
        self.d[name] = t
        return t

    def fm(self, name, T, nchunk, dt=F32, nfeat=None):
        t = self.sb(name, [128, nchunk * T], dt)
        return FM(t, T, nchunk, nfeat)

    def ps(self, name, shape, dt=F32):
        t = self.ctx.enter_context(self.nc.psum_tensor(name, list(shape), dt))
        self.d[name] = t
        return t

    def __getitem__(self, k):
        return self.d[k]


def cdiv(a, b):
    return (a + b - 1) // b


class Pool:

    def __init__(self, banks, sem):
        self.banks = banks
        self.sem = sem
        self.free_at = [None] * len(banks)
        self.i = 0

    def take(self):
        j = self.i % len(self.banks)
        self.i += 1
        fa = self.free_at[j]
        return j, self.banks[j], (0 if fa is None else fa[1])

    def take2(self):
        j = self.i % len(self.banks)
        self.i += 1
        return j, self.banks[j], self.free_at[j]

    def consumed(self, j, val, sem=None):
        self.free_at[j] = (sem or self.sem, val)


class Consts:
    def __init__(self, P, B):
        gs = P.sem("consts")
        self.eye = B.sb("eye_bf", [128, 128], BF16)
        self.eye_f = B.sb("eye_f", [128, 128], F32)
        self.ones_b = B.sb("ones_bf", [128, 1], BF16)
        self.ones_f = B.sb("ones_f", [128, 1], F32)
        self.ones_row = B.sb("ones_row_f", [1, 128], F32)
        self.ones_row_b = B.sb("ones_row_bf", [1, 128], BF16)
        P.inc("gpsimd", lambda e: e.memset(self.eye_f[:], 1.0), gs, 1)
        P.inc("gpsimd", lambda e: e.memset(self.ones_b[:], 1.0), gs, 1)
        P.inc("gpsimd", lambda e: e.memset(self.ones_f[:], 1.0), gs, 1)
        P.inc("gpsimd", lambda e: e.memset(self.ones_row[:], 1.0), gs, 1)
        P.inc("gpsimd", lambda e: e.memset(self.ones_row_b[:], 1.0), gs, 1)
        P.inc("gpsimd", lambda e: e.memset(self.eye[:], 1.0), gs, 1)
        P.wait("gpsimd", gs, gs.n)
        P.inc("gpsimd",
              lambda e: e.affine_select(self.eye[:], self.eye[:],
                                        pattern=[[-1, 128]],
                                        compare_op=ALU.is_equal, fill=0.0,
                                        base=0, channel_multiplier=1),
              gs, 1)
        P.wait("gpsimd", gs, gs.n)
        P.inc("gpsimd",
              lambda e: e.affine_select(self.eye_f[:], self.eye_f[:],
                                        pattern=[[-1, 128]],
                                        compare_op=ALU.is_equal, fill=0.0,
                                        base=0, channel_multiplier=1),
              gs, 1)
        self.gs = gs
        self.P = P
        self.cpool = B.sb("cpool", [128, 32], F32)
        self.cmap = {}

    @property
    def ready(self):
        return (self.gs, self.gs.n)

    def cbias(self, val):
        if val not in self.cmap:
            j = len(self.cmap)
            assert j < 32
            self.cmap[val] = j
            self.P.inc("gpsimd",
                       lambda e, j=j, val=val:
                       e.memset(self.cpool[:, j:j + 1], float(val)),
                       self.gs, 1)
        return self.cpool[:, self.cmap[val]:self.cmap[val] + 1]


class WT:

    def __init__(self, P, B, C, pb0, pb1):
        self.P, self.B, self.C = P, B, C
        self.stage = B.sb("wt_stage", [128, 1536], BF16)
        self.pb = [pb0, pb1]
        self.gld = P.sem("wt_gld")
        self.ts = P.sem("wt_ts")
        self.es = P.sem("wt_es")
        self.stage_free = 0
        self.bank_free = [0, 0]
        self.n = 0

    def load_T(self, name, w_dram, rows, cols, out_fm=None, war=None):
        P, B = self.P, self.B
        if out_fm is None:
            out_fm = B.fm(name + "_T", rows, cdiv(cols, 128), BF16, nfeat=cols)
        if war is not None:
            P.waitv("scalar", war)
        for rc in range(cdiv(rows, 128)):
            r0, r1 = 128 * rc, min(128 * (rc + 1), rows)
            rn = r1 - r0
            P.wait("gpsimd", self.es, self.stage_free)
            P.dma("gpsimd", self.stage[0:rn, 0:cols], w_dram[r0:r1, 0:cols],
                  self.gld)
            ld_done = self.gld.n
            for dc in range(cdiv(cols, 128)):
                d0, d1 = 128 * dc, min(128 * (dc + 1), cols)
                dn = d1 - d0
                j = self.n % 2
                bank = self.pb[j]
                self.n += 1
                P.wait("tensor", self.gld, ld_done)
                P.waitv("tensor", self.C.ready)
                P.wait("tensor", self.es, self.bank_free[j])
                P.inc("tensor",
                      lambda e, b=bank, rn=rn, d0=d0, dn=dn:
                      e.transpose(b[0:dn, 0:rn],
                                  self.stage[0:rn, d0:d0 + dn],
                                  self.C.eye[0:rn, 0:rn]),
                      self.ts, 1)
                P.wait("scalar", self.ts, self.ts.n)
                P.inc("scalar",
                      lambda e, b=bank, fm=out_fm, dc=dc, r0=r0, rn=rn, dn=dn:
                      e.copy(fm.chp(dc, 0, dn, r0, r0 + rn), b[0:dn, 0:rn]),
                      self.es, 1)
                self.bank_free[j] = self.es.n
            self.stage_free = self.es.n
        return out_fm, (self.es, self.es.n)


def load_w_nat(P, B, name, w_dram, d, f, gld, out_fm=None):
    if out_fm is None:
        out_fm = B.fm(name + "_N", f, cdiv(d, 128), BF16, nfeat=d)
    for dc in range(cdiv(d, 128)):
        d0, d1 = 128 * dc, min(128 * (dc + 1), d)
        P.dma("gpsimd", out_fm.chp(dc, 0, d1 - d0, 0, f), w_dram[d0:d1, 0:f],
              gld)
    return out_fm, (gld, gld.n)


def load_vec(P, B, name, v_dram, n, sem, dt=F32, eng="gpsimd"):
    t = B.sb(name, [128, cdiv(n, 128)], dt)
    for k in range(cdiv(n, 128)):
        a, b = 128 * k, min(128 * (k + 1), n)
        P.dma(eng, t[0:b - a, k:k + 1],
              v_dram[a:b].rearrange("(p one) -> p one", one=1), sem)
    return t


class LNCtx:
    def __init__(self, P, B, C, p_row, p_bm, p_br):
        self.P, self.B, self.C = P, B, C
        self.sq = B.sb("ln_sq", [128, 1024], BF16)
        self.rows = B.sb("ln_rows", [1, 4 * 512], F32)
        self.apl = B.sb("ln_apl", [128, 512], F32)
        self.p_row = p_row
        self.p_bm = p_bm
        self.p_br = p_br
        self.ts = P.sem("ln_ts")
        self.vs = P.sem("ln_vs")
        self.ss = P.sem("ln_ss")


def emit_ln(L, x, xn, T, src_ready, gain=None, eps=1e-5, nd=1024, keep=None,
            in_bf=False, t_base=0):
    P, C = L.P, L.C
    ones_x = C.ones_b if in_bf else C.ones_f
    ndc = nd // 128
    R = L.rows
    epsap = C.cbias(eps)
    c_ready = C.ready
    for ti, t0 in enumerate(range(0, T, 512)):
        t1 = min(t0 + 512, T)
        w = t1 - t0
        mean, msq, var, rstd = (R[:, 0:w], R[:, 512:512 + w],
                                R[:, 1024:1024 + w], R[:, 1536:1536 + w])
        p_row = L.p_row[0:1, 0:w]
        p_bm_row = L.p_bm[0:1, 0:w]
        p_bm = L.p_bm[:, 0:w]
        p_br = L.p_br[:, 0:w]
        for k in range(ndc):
            half = (k % 2) * 512
            sq = L.sq[:, half:half + w]
            xk = x.ch(k, t0, t1)
            P.waitv("scalar", src_ready)
            P.wait("scalar", L.ts, max(0, L.ts.n - 1))
            P.sinc("scalar",
                   lambda e, sq=sq, xk=xk: e.activation(sq, xk, AF.Square),
                   L.ss, 1)
            P.wait("tensor", L.ss, L.ss.n)
            if ti == 0 and k == 0:
                P.waitv("tensor", c_ready)
            P.inc("tensor",
                  lambda e, sq=sq, p=p_row, s=(k == 0), z=(k == ndc - 1):
                  e.matmul(p, C.ones_b[:, :], sq, start=s, stop=z),
                  L.ts, 1)
        P.wait("tensor", L.vs, L.vs.n)
        for k in range(ndc):
            xk = x.ch(k, t0, t1)
            P.inc("tensor",
                  lambda e, xk=xk, p=p_bm_row, s=(k == 0), z=(k == ndc - 1):
                  e.matmul(p, ones_x[:, :], xk, start=s, stop=z),
                  L.ts, 1)
        mm_done = L.ts.n
        P.wait("scalar", L.ts, mm_done)
        P.sinc("scalar",
               lambda e, o=mean, p=p_bm_row:
               e.activation(o, p, AF.Identity, scale=1.0 / nd), L.ss, 1)
        P.sinc("scalar",
               lambda e, o=msq, p=p_row:
               e.activation(o, p, AF.Identity, scale=1.0 / nd), L.ss, 1)
        P.wait("vector", L.ss, L.ss.n)
        P.sinc("vector",
               lambda e, o=var, a=mean: e.tensor_mul(o, a, a), L.vs, 1)
        P.sinc("vector",
               lambda e, o=var, a=msq, b=var: e.tensor_sub(o, a, b), L.vs, 1)
        P.wait("scalar", L.vs, L.vs.n)
        P.waitv("scalar", c_ready)
        P.sinc("scalar",
               lambda e, o=rstd, a=var, ep=epsap:
               e.activation(o, a, AF.Ln, bias=ep[0:1, :]), L.ss, 1)
        P.sinc("scalar",
               lambda e, o=rstd:
               e.activation(o, o, AF.Exp, scale=-0.5), L.ss, 1)
        if keep is not None:
            P.sinc("scalar",
                   lambda e, a=mean, o=keep[0:1, t_base + t0:t_base + t1]:
                   e.copy(o, a), L.ss, 1)
            P.sinc("scalar",
                   lambda e, a=rstd,
                   o=keep[0:1, T + t_base + t0:T + t_base + t1]:
                   e.copy(o, a), L.ss, 1)
        P.wait("tensor", L.ss, L.ss.n)
        P.wait("tensor", L.vs, L.vs.n)
        P.inc("tensor",
              lambda e, o=p_bm, a=mean:
              e.matmul(o, C.ones_row[:, :], a, start=True, stop=True),
              L.ts, 1)
        P.inc("tensor",
              lambda e, o=p_br, a=rstd:
              e.matmul(o, C.ones_row[:, :], a, start=True, stop=True),
              L.ts, 1)
        bc_done = L.ts.n
        P.wait("vector", L.ts, bc_done)
        for k in range(ndc):
            xk = x.ch(k, t0, t1)
            xo = xn.ch(k, t_base + t0, t_base + t1)
            apl = L.apl[:, 0:w]
            P.sinc("vector",
                   lambda e, o=apl, a=xk, b=p_bm: e.tensor_sub(o, a, b),
                   L.vs, 1)
            if gain is None:
                P.sinc("vector",
                       lambda e, o=xo, a=apl, b=p_br: e.tensor_mul(o, a, b),
                       L.vs, 1)
            else:
                g, bvec = gain
                gk = g[:, k:k + 1]
                bk = bvec[:, k:k + 1]
                P.sinc("vector",
                       lambda e, o=apl, b=p_br: e.tensor_mul(o, o, b),
                       L.vs, 1)
                P.sinc("vector",
                       lambda e, o=xo, a=apl, gk=gk, bk=bk:
                       e.tensor_scalar(o, a, gk, bk, ALU.mult, ALU.add),
                       L.vs, 1)
    return (L.vs, L.vs.n)



def mm_accum(P, mm_sem, bank, steps, extra_wait=()):
    for sv in extra_wait:
        P.waitv("tensor", sv)
    n = len(steps)
    for i, (lf, rf, mn) in enumerate(steps):
        last = i == n - 1
        def op(e, lf=lf, rf=rf, mn=mn, s=(i == 0), st=last, bank=bank):
            l = lf() if callable(lf) else lf
            r = rf() if callable(rf) else rf
            return e.matmul(bank[0:mn, 0:r.shape[1]], l, r, start=s, stop=st)
        if last:
            P.inc("tensor", op, mm_sem, 1)
        else:
            P.on("tensor", op)


def load_x(P, B, C, xin_dram, T, ld, name="x"):
    x = B.fm(name, T, 8, F32)
    for k in range(8):
        P.dma("sync", x.ch(k), xin_dram[128 * k:128 * (k + 1), :], ld)
    return x, (ld, ld.n)


def store_x(P, x, xout_dram, T, st, ready):
    for k in range(8):
        P.waitv("sync", ready)
        P.dma("sync", xout_dram[128 * k:128 * (k + 1), :], x.ch(k), st)
    P.wait("sync", st, st.n)


def emit_l1ffn(P, B, C, L, pool, wt, x, xn, T, dram, x_ready):
    mm = P.sem("ffn_mm")
    ev = pool.sem
    gld = P.sem("ffn_gld")
    wT, wT_rdy = wt.load_T("l1w", dram["L1_ffn_W"], 1024, 1024)
    vr, _ = load_w_nat(P, B, "l1vr", dram["L1_ffn_V_r"], 1024, 32, gld)
    cT, cT_rdy = wt.load_T("l1c", dram["L1_ffn_C"], 1024, 32)
    bvec = load_vec(P, B, "l1b", dram["L1_ffn_b"], 1024, gld)
    vr_rdy = b_rdy = (gld, gld.n)
    xn_rdy = emit_ln(L, x, xn, T, x_ready)
    xr = B.sb("l1xr", [32, T], BF16)
    for t0 in range(0, T, 512):
        t1 = min(t0 + 512, T)
        j, bank, free_at = pool.take2()
        P.waitt("tensor", free_at)
        mm_accum(P, mm, bank,
                 [(vr.ch(k), xn.ch(k, t0, t1), 32) for k in range(8)],
                 extra_wait=[vr_rdy, xn_rdy] if t0 == 0 else [])
        P.wait("vector", mm, mm.n)
        P.inc("vector",
              lambda e, bank=bank, t0=t0, t1=t1:
              e.tensor_copy(xr[0:32, t0:t1], bank[0:32, 0:t1 - t0]), ev, 1)
        pool.consumed(j, ev.n)
    xr_rdy = (ev, ev.n)
    for t0 in range(0, T, 512):
        t1 = min(t0 + 512, T)
        for oc in range(8):
            j, bank, free_at = pool.take2()
            P.waitt("tensor", free_at)
            steps = [(wT.ch(k)[:, 128 * oc:128 * (oc + 1)],
                      xn.ch(k, t0, t1), 128) for k in range(8)]
            steps.append((cT.chp(0, 0, 32, 128 * oc, 128 * (oc + 1)),
                          xr[0:32, t0:t1], 128))
            mm_accum(P, mm, bank, steps,
                     extra_wait=[wT_rdy, cT_rdy, xr_rdy] if (t0 == 0 and oc == 0) else [])
            P.wait("vector", mm, mm.n)
            if t0 == 0 and oc == 0:
                P.waitv("vector", b_rdy)
            P.inc("vector",
                  lambda e, bank=bank, oc=oc, t0=t0, t1=t1:
                  e.scalar_tensor_tensor(x.ch(oc, t0, t1),
                                         bank[0:128, 0:t1 - t0],
                                         bvec[:, oc:oc + 1],
                                         x.ch(oc, t0, t1),
                                         ALU.add, ALU.add), ev, 1)
            pool.consumed(j, ev.n)
    return (ev, ev.n)


def build_l1ffn_launch(T=2048):
    from contextlib import ExitStack
    nc = bass.Bass("TRN2", target_bir_lowering=False)
    dram = {}
    XIN = nc.dram_tensor("XIN", [1024, T], F32, kind="ExternalInput").ap()
    XOUT = nc.dram_tensor("XOUT", [1024, T], F32, kind="ExternalOutput").ap()
    for nm, shp in [("L1_ffn_W", [1024, 1024]), ("L1_ffn_b", [1024]),
                    ("L1_ffn_V_r", [1024, 32]), ("L1_ffn_C", [1024, 32])]:
        dram[nm] = nc.dram_tensor(nm, shp, F32, kind="ExternalInput").ap()
    ctx = ExitStack()
    with ctx:
        B = Bufs(nc, ctx)
        P = Prog(nc, ctx)
        C = Consts(P, B)
        p_row = B.ps("p_row", [1, 512], F32)
        p_bm = B.ps("p_bm", [128, 512], F32)
        p_br = B.ps("p_br", [128, 512], F32)
        pt0 = B.ps("pt0", [128, 128], BF16)
        pt1 = B.ps("pt1", [128, 128], BF16)
        pk = [B.ps(f"pk{i}", [128, 512], F32) for i in range(3)]
        ev = P.sem("ev")
        pool = Pool(pk, ev)
        wt = WT(P, B, C, pt0, pt1)
        L = LNCtx(P, B, C, p_row, p_bm, p_br)
        ld = P.sem("ld")
        st = P.sem("st")
        x, x_ready = load_x(P, B, C, XIN, T, ld)
        xn = B.fm("xn", T, 8, BF16)
        done = emit_l1ffn(P, B, C, L, pool, wt, x, xn, T, dram, x_ready)
        store_x(P, x, XOUT, T, st, done)
        P.emit()
    return nc

def load_bias_cols(P, B, name, v_dram, n, sem, eng="scalar", tile=None, col0=0,
                   dt=F32):
    ncol = cdiv(n, 128)
    if tile is None:
        tile = B.sb(name, [128, ncol], dt)
    for c in range(ncol):
        a, b = 128 * c, min(128 * (c + 1), n)
        P.dma(eng, tile[0:b - a, col0 + c:col0 + c + 1],
              v_dram[a:b].rearrange("(p one) -> p one", one=1), sem)
    return tile


def load_w_block(P, B, wbuf_fm, w_dram, d0, d1, f0, f1, sem, eng="gpsimd"):
    fw = f1 - f0
    Tw = wbuf_fm.T
    nchunk = (d1 - d0) // 128
    src = w_dram[d0:d1, f0:f1].rearrange("(c p) f -> p c f", p=128)
    dst = wbuf_fm.t[0:128, 0:nchunk * Tw].rearrange(
        "p (c f) -> p c f", f=Tw)[:, :, 0:fw]
    P.dma(eng, dst, src, sem)


def emit_l2ffn(P, B, C, L, pool, x, xn, T, dram, x_ready, units=None):
    HS, HR, E = 2048, 3072, 12
    mm = P.sem("l2f_mm")
    ev = pool.sem
    gld = P.sem("l2f_gld")
    bld = P.sem("l2f_bld")
    vs = pool.sem
    ss = P.sem("l2f_ss")
    ts = P.sem("l2f_ts")

    mrows = B.sb("l2f_mrows", [1, 2 * T], F32)
    xn_rdy = emit_ln(L, x, xn, T, x_ready, keep=mrows)

    bs1_t = load_bias_cols(P, B, "bs1_t", dram["L2_ffn_bs1"], 4096, bld)
    bs2_t = load_bias_cols(P, B, "bs2_t", dram["L2_ffn_bs2"], 1024, bld)
    be1_t = B.sb("be1_t", [128, 48 * E], F32)
    be2_t = B.sb("be2_t", [128, 8 * E], F32)
    for e in range(E):
        load_bias_cols(P, B, None, dram["L2_ffn_be1"][e, :], 6144, bld,
                       tile=be1_t, col0=48 * e)
        load_bias_cols(P, B, None, dram["L2_ffn_be2"][e, :], 1024, bld,
                       tile=be2_t, col0=8 * e)
    br_t = B.sb("br_t", [12, 1], F32)
    P.dma("scalar", br_t[:, :],
          dram["L2_ffn_br"].rearrange("(p one) -> p one", one=1), bld)
    bias_rdy = (bld, bld.n)

    act = B.fm("l2f_act", 1024, 24, BF16)
    gsc = B.sb("l2f_gsc", [128, 512], BF16)
    wga = B.fm("l2f_wga", 256, 8, BF16)
    wgg = B.fm("l2f_wgg", 256, 8, BF16)
    wdn = B.fm("l2f_wdn", 128, 24, BF16)
    wtm = B.sb("l2f_wtm", [128, 12], F32)
    wfm = B.sb("l2f_wfm", [12, T], BF16)
    cwb = B.sb("l2f_cwb", [128, 1024], BF16)
    m12 = B.sb("l2f_m12", [128, 4], F32)
    sel = B.sb("l2f_sel", [12, 12 * 128], BF16)
    gcs = P.sem("l2f_gcs")
    P.inc("gpsimd", lambda e_: e_.memset(sel[:], 1.0), gcs, 1)
    P.wait("gpsimd", gcs, gcs.n)
    P.inc("gpsimd",
          lambda e_: e_.affine_select(sel[:], sel[:], pattern=[[1, 12 * 128]],
                                      compare_op=ALU.is_ge, fill=0.0,
                                      base=0, channel_multiplier=-128),
          gcs, 1)
    P.wait("gpsimd", gcs, gcs.n)
    P.inc("gpsimd",
          lambda e_: e_.affine_select(sel[:], sel[:], pattern=[[-1, 12 * 128]],
                                      compare_op=ALU.is_ge, fill=0.0,
                                      base=127, channel_multiplier=128),
          gcs, 1)
    sel_rdy = (gcs, gcs.n)

    wrf = B.fm("l2wr", 12, 8, F32)
    for dc in range(8):
        P.dma("gpsimd", wrf.chp(dc, 0, 128, 0, 12),
              dram["L2_ffn_Wr"][128 * dc:128 * (dc + 1), 0:12], gld)
    br_row = B.sb("l2f_brrow", [1, 12], F32)
    P.dma("scalar", br_row[:, :],
          dram["L2_ffn_br"].rearrange("(one e) -> one e", one=1), bld)
    wr_rdy = (gld, gld.n)
    br_rdy = (bld, bld.n)
    cs_t = B.sb("l2f_cs", [128, 12], F32)
    csrow = B.sb("l2f_csrow", [1, 12], F32)
    brb = B.sb("l2f_brb", [128, 12], F32)
    nr = B.sb("l2f_nr", [128, 1], F32)
    j, bank, free_at = pool.take2()
    P.waitt("tensor", free_at)
    P.waitv("tensor", wr_rdy)
    P.waitv("tensor", C.ready)
    mm_accum(P, mm, bank, [(wrf.ch(k), C.ones_f[:, :], 12) for k in range(8)])
    P.wait("vector", mm, mm.n)
    P.sinc("vector", lambda e_, b=bank: e_.tensor_copy(m12[0:12, 3:4],
                                                       b[0:12, 0:1]), vs, 1)
    pool.consumed(j, vs.n)
    j, bank, free_at = pool.take2()
    P.waitt("tensor", free_at)
    P.wait("tensor", vs, vs.n)
    P.inc("tensor",
          lambda e_, b=bank: e_.transpose(b[0:1, 0:12], m12[0:12, 3:4],
                                          C.eye_f[0:12, 0:12]), mm, 1)
    P.wait("vector", mm, mm.n)
    P.sinc("vector", lambda e_, b=bank: e_.tensor_copy(csrow[0:1, 0:12],
                                                       b[0:1, 0:12]), vs, 1)
    pool.consumed(j, vs.n)
    j, bank, free_at = pool.take2()
    P.waitt("tensor", free_at)
    P.wait("tensor", vs, vs.n)
    P.inc("tensor",
          lambda e_, b=bank: e_.matmul(b[0:128, 0:12], C.ones_row[:, :],
                                       csrow[0:1, 0:12], start=True,
                                       stop=True), mm, 1)
    P.wait("vector", mm, mm.n)
    P.sinc("vector", lambda e_, b=bank: e_.tensor_copy(cs_t[:, :],
                                                       b[0:128, 0:12]), vs, 1)
    pool.consumed(j, vs.n)
    j, bank, free_at = pool.take2()
    P.waitt("tensor", free_at)
    P.waitv("tensor", br_rdy)
    P.inc("tensor",
          lambda e_, b=bank: e_.matmul(b[0:128, 0:12], C.ones_row[:, :],
                                       br_row[:, :], start=True, stop=True),
          mm, 1)
    P.wait("scalar", mm, mm.n)
    P.sinc("scalar",
           lambda e_, b=bank: e_.activation(brb[:, :], b[0:128, 0:12],
                                            AF.Identity, scale=1.0 / 1.5),
           ss, 1)
    P.wait("vector", ss, ss.n)
    pool.consumed(j, ss.n, ss)
    for tt in range(T // 128):
        t0 = 128 * tt
        j, bank, free_at = pool.take2()
        P.waitt("tensor", free_at)
        P.waitv("tensor", xn_rdy)
        mm_accum(P, mm, bank,
                 [(x.ch(k, t0, t0 + 128), wrf.ch(k), 128) for k in range(8)])
        raw_mm = mm.n
        mcol = L.apl[:, 64:65]
        rcol = L.apl[:, 65:66]
        j3, bank3, free_at3 = pool.take2()
        P.waitt("tensor", free_at3)
        P.inc("tensor",
              lambda e_, b=bank3, t0=t0:
              e_.transpose(b[0:128, 0:1], mrows[0:1, t0:t0 + 128],
                           C.eye_f[0:1, 0:1]), mm, 1)
        P.inc("tensor",
              lambda e_, b=bank3, t0=t0:
              e_.transpose(b[0:128, 1:2], mrows[0:1, T + t0:T + t0 + 128],
                           C.eye_f[0:1, 0:1]), mm, 1)
        P.wait("vector", mm, mm.n)
        P.sinc("vector",
               lambda e_, b=bank3: e_.tensor_copy(mcol, b[0:128, 0:1]), vs, 1)
        P.sinc("vector",
               lambda e_, b=bank3: e_.tensor_copy(rcol, b[0:128, 1:2]), vs, 1)
        pool.consumed(j3, vs.n)
        P.sinc("vector",
               lambda e_: e_.tensor_scalar(nr[:, 0:1], rcol, -1.0 / 1.5, None,
                                           ALU.mult), vs, 1)
        P.sinc("vector",
               lambda e_, b=bank: e_.scalar_tensor_tensor(
                   wtm[:, 0:12], cs_t[:, :], mcol, b[0:128, 0:12],
                   ALU.mult, ALU.subtract), vs, 1)
        pool.consumed(j, vs.n)
        P.sinc("vector",
               lambda e_: e_.tensor_scalar(wtm[:, 0:12], wtm[:, 0:12],
                                           nr[:, 0:1], None, ALU.mult), vs, 1)
        P.sinc("vector",
               lambda e_: e_.tensor_add(wtm[:, 0:12], wtm[:, 0:12], brb[:, :]),
               vs, 1)
        P.sinc("vector",
               lambda e_: e_.tensor_reduce(m12[:, 0:1], wtm[:, 0:12], AX.X,
                                           ALU.max), vs, 1)
        P.sinc("vector",
               lambda e_: e_.tensor_scalar(L.apl[:, 0:12], wtm[:, 0:12],
                                           m12[:, 0:1], None, ALU.is_ge), vs, 1)
        P.sinc("vector",
               lambda e_: e_.scalar_tensor_tensor(L.apl[:, 0:12], L.apl[:, 0:12],
                                                  -1e30, wtm[:, 0:12],
                                                  ALU.mult, ALU.add), vs, 1)
        P.sinc("vector",
               lambda e_: e_.tensor_reduce(m12[:, 1:2], L.apl[:, 0:12], AX.X,
                                           ALU.max), vs, 1)
        P.sinc("vector",
               lambda e_: e_.tensor_add(m12[:, 2:3], m12[:, 0:1], m12[:, 1:2]),
               vs, 1)
        P.sinc("vector",
               lambda e_: e_.tensor_scalar(L.apl[:, 0:12], wtm[:, 0:12], 2.0,
                                           m12[:, 2:3], ALU.mult, ALU.subtract),
               vs, 1)
        P.wait("scalar", vs, vs.n)
        P.sinc("scalar",
               lambda e_: e_.activation(L.apl[:, 16:28], L.apl[:, 0:12],
                                        AF.Sigmoid), ss, 1)
        P.sinc("vector",
               lambda e_: e_.tensor_scalar(L.apl[:, 32:44], wtm[:, 0:12],
                                           m12[:, 1:2], None, ALU.is_ge), vs, 1)
        P.wait("vector", ss, ss.n)
        P.sinc("vector",
               lambda e_: e_.tensor_mul(wtm[:, 0:12], L.apl[:, 16:28],
                                        L.apl[:, 32:44]), vs, 1)
        j2, bank2, free_at2 = pool.take2()
        P.waitt("tensor", free_at2)
        P.wait("tensor", vs, vs.n)
        P.inc("tensor",
              lambda e_, b=bank2, t0=t0:
              e_.transpose(b[0:12, 0:128], wtm[:, 0:12], C.eye_f[:, :]),
              mm, 1)
        P.wait("vector", mm, mm.n)
        P.sinc("vector",
               lambda e_, b=bank2, t0=t0:
               e_.tensor_copy(wfm[0:12, t0:t0 + 128], b[0:12, 0:128]), vs, 1)
        pool.consumed(j2, vs.n)
    wfm_rdy = (vs, vs.n)

    for half in range(2):
        h0 = 1024 * half
        for ex in (range(-1, E) if units is None else units):
            plain = isinstance(ex, str) and ex.endswith("s")
            wmode = isinstance(ex, str) and ex.endswith("w")
            if isinstance(ex, str):
                ex = int(ex[:-1])
            if ex < 0:
                hr = HS
                w1 = dram["L2_ffn_Ws1"]
                w2 = dram["L2_ffn_Ws2"]
                b1t, b1c0 = bs1_t, 0
                b2t, b2c0 = bs2_t, 0
            else:
                hr = HR
                w1 = dram["L2_ffn_We1"][ex]
                w2 = dram["L2_ffn_We2"][ex]
                b1t, b1c0 = be1_t, 48 * ex
                b2t, b2c0 = be2_t, 8 * ex
                for q in ([] if plain else range(2)):
                    t0 = h0 + 512 * q
                    jq, bq, fq = pool.take2()
                    P.waitt("tensor", fq)
                    if ex == 0 and half == 0 and q == 0:
                        P.waitv("tensor", wfm_rdy)
                    P.waitv("tensor", sel_rdy)
                    P.inc("tensor",
                          lambda e_, b=bq, ex=ex, t0=t0:
                          e_.matmul(b[0:128, 0:512],
                                    sel[0:12, 128 * ex:128 * (ex + 1)],
                                    wfm[0:12, t0:t0 + 512],
                                    start=True, stop=True), mm, 1)
                    P.wait("vector", mm, mm.n)
                    P.sinc("vector",
                           lambda e_, b=bq, q=q:
                           e_.tensor_copy(cwb[:, 512 * q:512 * q + 512],
                                          b[0:128, 0:512]), vs, 1)
                    pool.consumed(jq, vs.n)
            nj = hr // 128
            for jg in range(0, nj, 2):
                f0 = 128 * jg
                fw = min(256, hr - f0)
                P.wait("gpsimd", mm, mm.n)
                load_w_block(P, B, wga, w1, 0, 1024, f0, f0 + fw, gld)
                load_w_block(P, B, wgg, w1, 0, 1024, hr + f0, hr + f0 + fw, gld)
                ld_done = gld.n
                for jj in range(fw // 128):
                    jch = jg + jj
                    for q in range(2):
                        t0 = h0 + 512 * q
                        ja, ba, fa = pool.take2()
                        P.waitt("tensor", fa)
                        P.wait("tensor", gld, ld_done)
                        mm_accum(P, mm, ba,
                                 [(wga.chp(k, 0, 128, 128 * jj, 128 * jj + 128),
                                   xn.ch(k, t0, t0 + 512), 128)
                                  for k in range(8)])
                        a_mm = mm.n
                        jgm, bg, fg = pool.take2()
                        P.waitt("tensor", fg)
                        mm_accum(P, mm, bg,
                                 [(wgg.chp(k, 0, 128, 128 * jj, 128 * jj + 128),
                                   xn.ch(k, t0, t0 + 512), 128)
                                  for k in range(8)])
                        g_mm = mm.n
                        P.wait("scalar", mm, g_mm)
                        P.wait("scalar", vs, vs.n)
                        P.sinc("scalar",
                               lambda e_, b=bg, c=b1c0 + nj + jch, b1t=b1t:
                               e_.activation(gsc[:, 0:512], b[0:128, 0:512],
                                             AF.Gelu, bias=b1t[:, c:c + 1]),
                               ss, 1)
                        P.wait("vector", ss, ss.n)
                        P.wait("vector", mm, a_mm)
                        dst = act.ch(jch, 512 * q, 512 * q + 512)
                        P.sinc("vector",
                               lambda e_, b=ba, c=b1c0 + jch, dst=dst, b1t=b1t:
                               e_.scalar_tensor_tensor(dst, b[0:128, 0:512],
                                                       b1t[:, c:c + 1],
                                                       gsc[:, 0:512],
                                                       ALU.add, ALU.mult),
                               vs, 1)
                        if ex >= 0 and not plain:
                            P.sinc("vector",
                                   lambda e_, dst=dst, q=q:
                                   e_.tensor_mul(dst, dst,
                                                 cwb[:, 512 * q:512 * q + 512]),
                                   vs, 1)
                        pool.consumed(ja, vs.n)
                        pool.consumed(jgm, vs.n)
            for og in range(8):
                o0 = 128 * og
                wd = wdn
                P.wait("gpsimd", mm, mm.n)
                load_w_block(P, B, wd, w2, 0, hr, o0, o0 + 128, gld)
                ld_done = gld.n
                for oc2 in range(1):
                    oc = og
                    for q in range(2):
                        t0 = h0 + 512 * q
                        jd, bd, fd = pool.take2()
                        P.waitt("tensor", fd)
                        P.wait("tensor", gld, ld_done)
                        P.wait("tensor", vs, vs.n)
                        mm_accum(P, mm, bd,
                                 [(wd.chp(k, 0, 128, 0, 128),
                                   act.ch(k, 512 * q, 512 * q + 512), 128)
                                  for k in range(nj)])
                        P.wait("vector", mm, mm.n)
                        dst = x.ch(oc, t0, t0 + 512)
                        if ex < 0 or plain or wmode:
                            P.sinc("vector",
                                   lambda e_, b=bd, c=b2c0 + oc, dst=dst,
                                   b2t=b2t:
                                   e_.scalar_tensor_tensor(dst, b[0:128, 0:512],
                                                           b2t[:, c:c + 1], dst,
                                                           ALU.add, ALU.add),
                                   vs, 1)
                        else:
                            P.sinc("vector",
                                   lambda e_, c=b2c0 + oc, q=q, dst=dst,
                                   b2t=b2t:
                                   e_.scalar_tensor_tensor(
                                       L.apl[:, 0:512],
                                       cwb[:, 512 * q:512 * q + 512],
                                       b2t[:, c:c + 1], dst,
                                       ALU.mult, ALU.add), vs, 1)
                            P.sinc("vector",
                                   lambda e_, b=bd, dst=dst:
                                   e_.tensor_add(dst, L.apl[:, 0:512],
                                                 b[0:128, 0:512]), vs, 1)
                        pool.consumed(jd, vs.n)
    return (vs, vs.n)

SHAPES = {
    "L1_ffn_W": [1024, 1024], "L1_ffn_b": [1024],
    "L1_ffn_V_r": [1024, 32], "L1_ffn_C": [1024, 32],
    "L2_ffn_Ws1": [1024, 4096], "L2_ffn_bs1": [4096],
    "L2_ffn_Ws2": [2048, 1024], "L2_ffn_bs2": [1024],
    "L2_ffn_We1": [12, 1024, 6144], "L2_ffn_be1": [12, 6144],
    "L2_ffn_We2": [12, 3072, 1024], "L2_ffn_be2": [12, 1024],
    "L2_ffn_Wr": [1024, 12], "L2_ffn_br": [12],
    "L0_attn_w_qkv": [1024, 3072], "L0_attn_w_out": [1024, 1024],
    "L0_attn_head_scales": [8],
    "L0_ffn_gate_weight": [8, 1024],
    "L0_ffn_shared_gate_up": [1876, 1024], "L0_ffn_shared_down": [1024, 938],
    "L0_ffn_experts_gate_up": [8, 1876, 1024],
    "L0_ffn_experts_down": [8, 1024, 938],
    "L1_attn_ln1_g": [1024], "L1_attn_ln1_b": [1024],
    "L1_attn_ln2_g": [1024], "L1_attn_ln2_b": [1024],
    "L1_attn_W_qkv": [1024, 3072], "L1_attn_W_O": [1024, 1024],
    "L1_attn_b_O": [1024], "L1_attn_W_ug": [1024, 1152],
    "L1_attn_b_ug": [1152], "L1_attn_B_w": [1024, 128],
    "L1_attn_A": [128], "L1_attn_C_w": [128, 1024],
    "L1_attn_mlp_W1": [1024, 4096], "L1_attn_mlp_b1": [4096],
    "L1_attn_mlp_W2": [4096, 1024], "L1_attn_mlp_b2": [1024],
    "L2_attn_M_init": [1024, 1024], "L2_attn_eta_w": [1024],
    "L2_attn_eta_b": [1], "L2_attn_alpha_w": [1024], "L2_attn_alpha_b": [1],
    "L2_attn_gate_w": [1024], "L2_attn_gate_b": [1],
    "L2_attn_vg_w1": [1024, 1024], "L2_attn_vg_w2": [1024, 1024],
    "L3_attn_in_proj_weight": [3072, 1024], "L3_attn_out_weight": [1024, 1024],
    "L3_ffn_Wq": [1024, 2048], "L3_ffn_bq": [2048],
    "L3_ffn_bn_gamma": [256], "L3_ffn_bn_beta": [256],
    "L3_ffn_sub_keys0": [128, 128], "L3_ffn_sub_keys1": [128, 128],
    "L3_ffn_expert_down": [16384, 1024], "L3_ffn_expert_up": [16384, 1024],
}


def make_launch(in_names, emit_body, T=2048, npool=3, extra=None):
    from contextlib import ExitStack
    nc = bass.Bass("TRN2", target_bir_lowering=False)
    XIN = nc.dram_tensor("XIN", [1024, T], F32, kind="ExternalInput").ap()
    XOUT = nc.dram_tensor("XOUT", [1024, T], F32, kind="ExternalOutput").ap()
    dram = {nm: nc.dram_tensor(nm, SHAPES[nm], F32, kind="ExternalInput").ap()
            for nm in in_names}
    ctx = ExitStack()
    with ctx:
        B = Bufs(nc, ctx)
        P = Prog(nc, ctx)
        C = Consts(P, B)
        p_row = B.ps("p_row", [1, 512], F32)
        p_bm = B.ps("p_bm", [128, 512], F32)
        p_br = B.ps("p_br", [128, 512], F32)
        ev = P.sem("ev")
        pk = [B.ps(f"pk{i}", [128, 512], F32) for i in range(npool)]
        pool = Pool(pk, ev)
        L = LNCtx(P, B, C, p_row, p_bm, p_br)
        ld = P.sem("ld")
        st = P.sem("st")
        x, x_ready = load_x(P, B, C, XIN, T, ld)
        xn = B.fm("xn", T, 8, BF16)
        env = dict(nc=nc, B=B, P=P, C=C, L=L, pool=pool, x=x, xn=xn, T=T,
                   dram=dram, x_ready=x_ready, ctx=ctx)
        if extra:
            extra(env)
        done = emit_body(env)
        store_x(P, x, XOUT, T, st, done)
        P.emit()
    return nc


def build_l2ffn_launch(T=2048, units=None):
    names = [n for n in SHAPES if n.startswith("L2_ffn")]
    def body(env):
        return emit_l2ffn(env["P"], env["B"], env["C"], env["L"], env["pool"],
                          env["x"], env["xn"], env["T"], env["dram"],
                          env["x_ready"], units=units)
    return make_launch(names, body, T, npool=5)

def emit_l0ffn(P, B, C, L, pool, wt, x, xn, T, dram, x_ready):
    E, I = 8, 938
    F2 = 2 * I
    mm = P.sem("l0f_mm")
    vs = pool.sem
    ss = P.sem("l0f_ss")
    gld = P.sem("l0f_gld")
    bld = P.sem("l0f_bld")
    guT = B.fm("l0f_guT", F2, 8, BF16, nfeat=1024)
    dT_early = None
    dT = B.fm("l0f_dT", 1024, 8, BF16, nfeat=I)
    act = B.fm("l0f_act", T, 8, BF16)
    mrows = act.t[0:1, 0:8192].bitcast(F32)
    xn_rdy = emit_ln(L, x, xn, T, x_ready, keep=mrows)
    gsc = B.sb("l0f_gsc", [128, 512], BF16)
    cwb = B.sb("l0f_cwb", [128, T], BF16)
    wtm = B.sb("l0f_wtm", [128, 8], F32)
    wfm = B.sb("l0f_wfm", [8, T], BF16)
    m12 = B.sb("l0f_m12", [128, 8], F32)
    nr = B.sb("l0f_nr", [128, 2], F32)
    gw = B.sb("l0f_gw", [8, 1024], F32)
    gwT = B.fm("l0f_gwT", 8, 8, F32)
    cs_t = B.sb("l0f_cs", [128, 8], F32)
    csrow = B.sb("l0f_csrow", [1, 8], F32)
    sel = B.sb("l0f_sel", [8, 8 * 128], BF16)
    gcs = P.sem("l0f_gcs")
    P.inc("gpsimd", lambda e_: e_.memset(sel[:], 1.0), gcs, 1)
    P.wait("gpsimd", gcs, gcs.n)
    P.inc("gpsimd",
          lambda e_: e_.affine_select(sel[:], sel[:], pattern=[[1, 8 * 128]],
                                      compare_op=ALU.is_ge, fill=0.0,
                                      base=0, channel_multiplier=-128),
          gcs, 1)
    P.wait("gpsimd", gcs, gcs.n)
    P.inc("gpsimd",
          lambda e_: e_.affine_select(sel[:], sel[:], pattern=[[-1, 8 * 128]],
                                      compare_op=ALU.is_ge, fill=0.0,
                                      base=127, channel_multiplier=128),
          gcs, 1)
    sel_rdy = (gcs, gcs.n)

    P.dma("sync", gw[:, :], dram["L0_ffn_gate_weight"][:, :], bld)
    gw_rdy = (bld, bld.n)
    for dc in range(8):
        j, bank, free_at = pool.take2()
        P.waitt("tensor", free_at)
        P.waitv("tensor", gw_rdy)
        P.waitv("tensor", C.ready)
        P.inc("tensor",
              lambda e_, b=bank, dc=dc:
              e_.transpose(b[0:128, 0:8], gw[0:8, 128 * dc:128 * (dc + 1)],
                           C.eye_f[0:8, 0:8]), mm, 1)
        P.wait("vector", mm, mm.n)
        P.sinc("vector",
               lambda e_, b=bank, dc=dc:
               e_.tensor_copy(gwT.chp(dc, 0, 128, 0, 8), b[0:128, 0:8]), vs, 1)
        pool.consumed(j, vs.n)
    j, bank, free_at = pool.take2()
    P.waitt("tensor", free_at)
    P.wait("tensor", vs, vs.n)
    mm_accum(P, mm, bank, [(gwT.ch(k), C.ones_f[:, :], 8) for k in range(8)])
    P.wait("vector", mm, mm.n)
    P.sinc("vector", lambda e_, b=bank: e_.tensor_copy(m12[0:8, 0:1],
                                                       b[0:8, 0:1]), vs, 1)
    pool.consumed(j, vs.n)
    j, bank, free_at = pool.take2()
    P.waitt("tensor", free_at)
    P.wait("tensor", vs, vs.n)
    P.inc("tensor",
          lambda e_, b=bank: e_.transpose(b[0:1, 0:8], m12[0:8, 0:1],
                                          C.eye_f[0:8, 0:8]), mm, 1)
    P.wait("vector", mm, mm.n)
    P.sinc("vector", lambda e_, b=bank: e_.tensor_copy(csrow[0:1, 0:8],
                                                       b[0:1, 0:8]), vs, 1)
    pool.consumed(j, vs.n)
    j, bank, free_at = pool.take2()
    P.waitt("tensor", free_at)
    P.wait("tensor", vs, vs.n)
    P.inc("tensor",
          lambda e_, b=bank: e_.matmul(b[0:128, 0:8], C.ones_row[:, :],
                                       csrow[0:1, 0:8], start=True, stop=True),
          mm, 1)
    P.wait("vector", mm, mm.n)
    P.sinc("vector", lambda e_, b=bank: e_.tensor_copy(cs_t[:, :],
                                                       b[0:128, 0:8]), vs, 1)
    pool.consumed(j, vs.n)
    for tt in range(T // 128):
        t0 = 128 * tt
        j, bank, free_at = pool.take2()
        P.waitt("tensor", free_at)
        P.waitv("tensor", xn_rdy)
        mm_accum(P, mm, bank,
                 [(x.ch(k, t0, t0 + 128), gwT.ch(k), 128) for k in range(8)])
        mcol = L.apl[:, 64:65]
        rcol = L.apl[:, 65:66]
        j3, bank3, free_at3 = pool.take2()
        P.waitt("tensor", free_at3)
        P.inc("tensor",
              lambda e_, b=bank3, t0=t0:
              e_.transpose(b[0:128, 0:1], mrows[0:1, t0:t0 + 128],
                           C.eye_f[0:1, 0:1]), mm, 1)
        P.inc("tensor",
              lambda e_, b=bank3, t0=t0:
              e_.transpose(b[0:128, 1:2], mrows[0:1, T + t0:T + t0 + 128],
                           C.eye_f[0:1, 0:1]), mm, 1)
        P.wait("vector", mm, mm.n)
        P.sinc("vector",
               lambda e_, b=bank3: e_.tensor_copy(mcol, b[0:128, 0:1]), vs, 1)
        P.sinc("vector",
               lambda e_, b=bank3: e_.tensor_copy(rcol, b[0:128, 1:2]), vs, 1)
        pool.consumed(j3, vs.n)
        P.sinc("vector",
               lambda e_, b=bank: e_.scalar_tensor_tensor(
                   wtm[:, 0:8], cs_t[:, :], mcol, b[0:128, 0:8],
                   ALU.mult, ALU.subtract), vs, 1)
        pool.consumed(j, vs.n)
        P.sinc("vector",
               lambda e_: e_.tensor_scalar(nr[:, 0:1], rcol, -1.0, None,
                                           ALU.mult), vs, 1)
        P.sinc("vector",
               lambda e_: e_.tensor_scalar(wtm[:, 0:8], wtm[:, 0:8],
                                           nr[:, 0:1], None, ALU.mult), vs, 1)
        P.sinc("vector",
               lambda e_: e_.tensor_reduce(m12[:, 0:1], wtm[:, 0:8], AX.X,
                                           ALU.max), vs, 1)
        P.sinc("vector",
               lambda e_: e_.tensor_scalar(m12[:, 3:4], m12[:, 0:1], -1.0,
                                           None, ALU.mult), vs, 1)
        P.wait("scalar", vs, vs.n)
        P.sinc("scalar",
               lambda e_: e_.activation(L.apl[:, 0:8], wtm[:, 0:8], AF.Exp,
                                        bias=m12[:, 3:4]), ss, 1)
        P.wait("vector", ss, ss.n)
        P.sinc("vector",
               lambda e_: e_.tensor_reduce(m12[:, 4:5], L.apl[:, 0:8], AX.X,
                                           ALU.add), vs, 1)
        P.sinc("vector",
               lambda e_: e_.reciprocal(m12[:, 5:6], m12[:, 4:5]), vs, 1)
        P.sinc("vector",
               lambda e_: e_.tensor_scalar(L.apl[:, 16:24], L.apl[:, 0:8],
                                           m12[:, 5:6], None, ALU.mult), vs, 1)
        P.sinc("vector",
               lambda e_: e_.tensor_reduce(m12[:, 0:1], L.apl[:, 16:24], AX.X,
                                           ALU.max), vs, 1)
        P.sinc("vector",
               lambda e_: e_.tensor_scalar(L.apl[:, 32:40], L.apl[:, 16:24],
                                           m12[:, 0:1], None, ALU.is_ge), vs, 1)
        P.sinc("vector",
               lambda e_: e_.scalar_tensor_tensor(L.apl[:, 32:40],
                                                  L.apl[:, 32:40], -1e30,
                                                  L.apl[:, 16:24],
                                                  ALU.mult, ALU.add), vs, 1)
        P.sinc("vector",
               lambda e_: e_.tensor_reduce(m12[:, 1:2], L.apl[:, 32:40], AX.X,
                                           ALU.max), vs, 1)
        P.sinc("vector",
               lambda e_: e_.tensor_add(m12[:, 2:3], m12[:, 0:1], m12[:, 1:2]),
               vs, 1)
        P.sinc("vector",
               lambda e_: e_.tensor_scalar(m12[:, 2:3], m12[:, 2:3], 1e-8,
                                           None, ALU.add), vs, 1)
        P.sinc("vector",
               lambda e_: e_.reciprocal(m12[:, 6:7], m12[:, 2:3]), vs, 1)
        P.sinc("vector",
               lambda e_: e_.tensor_scalar(L.apl[:, 40:48], L.apl[:, 16:24],
                                           m12[:, 1:2], None, ALU.is_ge), vs, 1)
        P.sinc("vector",
               lambda e_: e_.scalar_tensor_tensor(wtm[:, 0:8], L.apl[:, 16:24],
                                                  m12[:, 6:7], L.apl[:, 40:48],
                                                  ALU.mult, ALU.mult), vs, 1)
        j2, bank2, free_at2 = pool.take2()
        P.waitt("tensor", free_at2)
        P.wait("tensor", vs, vs.n)
        P.inc("tensor",
              lambda e_, b=bank2, t0=t0:
              e_.transpose(b[0:8, 0:128], wtm[:, 0:8], C.eye_f[:, :]), mm, 1)
        P.wait("vector", mm, mm.n)
        P.sinc("vector",
               lambda e_, b=bank2, t0=t0:
               e_.tensor_copy(wfm[0:8, t0:t0 + 128], b[0:8, 0:128]), vs, 1)
        pool.consumed(j2, vs.n)
    wfm_rdy = (vs, vs.n)

    for ex in range(-1, E):
        if ex < 0:
            w1 = dram["L0_ffn_shared_gate_up"]
            w2 = dram["L0_ffn_shared_down"]
        else:
            w1 = dram["L0_ffn_experts_gate_up"][ex]
            w2 = dram["L0_ffn_experts_down"][ex]
            for q in range(T // 512):
                t0 = 512 * q
                jq, bq, fq = pool.take2()
                P.waitt("tensor", fq)
                if ex == 0 and q == 0:
                    P.waitv("tensor", wfm_rdy)
                    P.waitv("tensor", sel_rdy)
                P.inc("tensor",
                      lambda e_, b=bq, ex=ex, t0=t0:
                      e_.matmul(b[0:128, 0:512],
                                sel[0:8, 128 * ex:128 * (ex + 1)],
                                wfm[0:8, t0:t0 + 512],
                                start=True, stop=True), mm, 1)
                P.wait("vector", mm, mm.n)
                P.sinc("vector",
                       lambda e_, b=bq, t0=t0:
                       e_.tensor_copy(cwb[:, t0:t0 + 512], b[0:128, 0:512]),
                       vs, 1)
                pool.consumed(jq, vs.n)
        wt.load_T("l0gu", w1, F2, 1024, out_fm=guT, war=(mm, mm.n))
        gu_rdy = (wt.es, wt.es.n)
        for jch in range(8):
            gm = min(128, I - 128 * jch)
            for t0 in range(0, T, 512):
                t1 = t0 + 512
                ja, ba, fa = pool.take2()
                P.waitt("tensor", fa)
                if jch == 0 and t0 == 0:
                    P.waitv("tensor", gu_rdy)
                mm_accum(P, mm, ba,
                         [(guT.chp(k, 0, 128, 128 * jch, 128 * jch + gm),
                           xn.ch(k, t0, t1), gm) for k in range(8)])
                g_mm = mm.n
                jb, bb, fb = pool.take2()
                P.waitt("tensor", fb)
                mm_accum(P, mm, bb,
                         [(guT.chp(k, 0, 128, I + 128 * jch, I + 128 * jch + gm),
                           xn.ch(k, t0, t1), gm) for k in range(8)])
                u_mm = mm.n
                P.wait("scalar", mm, g_mm)
                P.wait("scalar", vs, vs.n)
                P.sinc("scalar",
                       lambda e_, b=ba, gm=gm:
                       e_.activation(gsc[0:gm, 0:512], b[0:gm, 0:512],
                                     AF.Silu), ss, 1)
                P.wait("vector", ss, ss.n)
                P.wait("vector", mm, u_mm)
                dst = act.chp(jch, 0, gm, t0, t1)
                P.sinc("vector",
                       lambda e_, b=bb, gm=gm, dst=dst:
                       e_.scalar_tensor_tensor(dst, b[0:gm, 0:512], 1.0,
                                               gsc[0:gm, 0:512],
                                               ALU.mult, ALU.mult), vs, 1)
                if ex >= 0:
                    P.sinc("vector",
                           lambda e_, dst=dst, gm=gm, t0=t0, t1=t1:
                           e_.tensor_mul(dst, dst, cwb[0:gm, t0:t1]), vs, 1)
                pool.consumed(ja, vs.n)
                pool.consumed(jb, vs.n)
        wt.load_T("l0dn", w2, 1024, I, out_fm=dT, war=(mm, mm.n))
        dn_rdy = (wt.es, wt.es.n)
        for oc in range(8):
            for t0 in range(0, T, 512):
                t1 = t0 + 512
                jd, bd, fd = pool.take2()
                P.waitt("tensor", fd)
                if oc == 0 and t0 == 0:
                    P.waitv("tensor", dn_rdy)
                P.wait("tensor", vs, vs.n)
                steps = []
                for k in range(8):
                    km = min(128, I - 128 * k)
                    steps.append((dT.chp(k, 0, km, 128 * oc, 128 * (oc + 1)),
                                  act.chp(k, 0, km, t0, t1), 128))
                mm_accum(P, mm, bd, steps)
                P.wait("vector", mm, mm.n)
                dst = x.ch(oc, t0, t1)
                P.sinc("vector",
                       lambda e_, b=bd, dst=dst:
                       e_.tensor_add(dst, dst, b[0:128, 0:512]), vs, 1)
                pool.consumed(jd, vs.n)
    return (vs, vs.n)


def build_l0ffn_launch(T=2048):
    names = [n for n in SHAPES if n.startswith("L0_ffn")]
    def body(env):
        B, P, C = env["B"], env["P"], env["C"]
        pt0 = B.ps("pt0", [128, 128], BF16)
        pt1 = B.ps("pt1", [128, 128], BF16)
        wt = WT(P, B, C, pt0, pt1)
        return emit_l0ffn(P, B, C, env["L"], env["pool"], wt,
                          env["x"], env["xn"], env["T"], env["dram"],
                          env["x_ready"])
    return make_launch(names, body, T, npool=3)

def emit_l0attn(P, B, C, L, pool, x, xn, T, dram, x_ready):
    H, DH = 8, 128
    mm = P.sem("l0a_mm")
    vs = pool.sem
    ss = P.sem("l0a_ss")
    gld = P.sem("l0a_gld")
    bld = P.sem("l0a_bld")
    xn_rdy = emit_ln(L, x, xn, T, x_ready)

    qf = B.fm("l0a_qf", T, 1, BF16)
    kf = B.fm("l0a_kf", T, 1, BF16)
    vf = B.fm("l0a_vf", T, 1, BF16)
    vtm = B.fm("l0a_vtm", 128, T // 128, BF16)
    ssb = B.sb("l0a_ssb", [128, T], F32)
    psb = B.sb("l0a_psb", [128, T], BF16)
    qp = B.sb("l0a_qp", [1, 2 * T], BF16)
    lrow = B.sb("l0a_lrow", [128, 2], F32)
    atn = B.fm("l0a_atn", T, 8, BF16)
    wq = B.fm("l0a_wq", 128, 8, BF16, nfeat=1024)
    wk = B.fm("l0a_wk", 128, 8, BF16, nfeat=1024)
    wv = B.fm("l0a_wv", 128, 8, BF16, nfeat=1024)
    hsr = B.sb("l0a_hs", [1, 8], F32)
    dirs = B.sb("l0a_dirs", [3, 8], BF16)
    drow = B.sb("l0a_drow", [1, 8], F32)
    dirs_d = P.nc.dram_tensor("l0a_dirs_d", [3, 8], F32, kind="Internal").ap()
    cmask = B.sb("l0a_cmask", [128, 128], BF16)
    gcs = P.sem("l0a_gcs")
    st2 = P.sem("l0a_st2")

    P.inc("gpsimd", lambda e_: e_.memset(cmask[:], 0.0), gcs, 1)
    P.wait("gpsimd", gcs, gcs.n)
    P.inc("gpsimd",
          lambda e_: e_.affine_select(cmask[:], cmask[:], pattern=[[-1, 128]],
                                      compare_op=ALU.is_ge, fill=-1e30,
                                      base=0, channel_multiplier=1),
          gcs, 1)
    mask_rdy = (gcs, gcs.n)
    P.dma("sync", hsr[:, :],
          dram["L0_attn_head_scales"].rearrange("(one e) -> one e", one=1), bld)
    hs_rdy = (bld, bld.n)
    P.inc("gpsimd",
          lambda e_: e_.iota(drow[:, :], pattern=[[1, 8]], base=0,
                             channel_multiplier=0,
                             allow_small_or_imprecise_dtypes=True), gcs, 1)
    P.wait("vector", gcs, gcs.n)
    s3 = 1.0 / np.sqrt(3.0)
    scr = B.sb("l0a_dscr", [1, 48], F32)
    dri = scr[0:1, 24:32].bitcast(I32)
    P.sinc("vector", lambda e_: e_.tensor_copy(dri[0:1, 0:8], drow[:, :]),
           vs, 1)
    P.sinc("vector", lambda e_: e_.tensor_scalar(dri[0:1, 0:8], dri[0:1, 0:8],
                                                 3, None, ALU.bitwise_and),
           vs, 1)
    P.sinc("vector", lambda e_: e_.tensor_copy(scr[0:1, 0:8], dri[0:1, 0:8]),
           vs, 1)
    P.sinc("vector", lambda e_: e_.tensor_copy(dri[0:1, 0:8], drow[:, :]),
           vs, 1)
    P.sinc("vector", lambda e_: e_.tensor_scalar(dri[0:1, 0:8], dri[0:1, 0:8],
                                                 1, None, ALU.bitwise_and),
           vs, 1)
    P.sinc("vector", lambda e_: e_.tensor_copy(scr[0:1, 8:16], dri[0:1, 0:8]),
           vs, 1)
    P.sinc("vector", lambda e_: e_.tensor_sub(scr[0:1, 16:24], drow[:, :],
                                              scr[0:1, 0:8]), vs, 1)
    P.sinc("vector", lambda e_: e_.tensor_scalar(scr[0:1, 16:24], scr[0:1, 16:24],
                                                 2.0 * s3 / 4.0, -s3,
                                                 ALU.mult, ALU.add), vs, 1)
    P.sinc("vector", lambda e_: e_.tensor_sub(scr[0:1, 24:32], scr[0:1, 0:8],
                                              scr[0:1, 8:16]), vs, 1)
    P.sinc("vector", lambda e_: e_.tensor_scalar(scr[0:1, 24:32], scr[0:1, 24:32],
                                                 2.0 * s3 / 2.0, -s3,
                                                 ALU.mult, ALU.add), vs, 1)
    P.sinc("vector", lambda e_: e_.tensor_scalar(scr[0:1, 32:40], scr[0:1, 8:16],
                                                 2.0 * s3, -s3,
                                                 ALU.mult, ALU.add), vs, 1)
    rows_done = vs.n
    P.wait("sync", vs, rows_done)
    P.dma("sync", dirs_d[0:1, :], scr[0:1, 16:24], st2)
    P.dma("sync", dirs_d[1:2, :], scr[0:1, 24:32], st2)
    P.dma("sync", dirs_d[2:3, :], scr[0:1, 32:40], st2)
    P.wait("gpsimd", st2, st2.n)
    dld = P.sem("l0a_dld")
    P.dma("gpsimd", dirs[:, :], dirs_d[:, :], dld)
    dirs_rdy = (dld, dld.n)
    wo, _ = load_w_nat(P, B, "l0a_wo", dram["L0_attn_w_out"], 1024, 1024, gld)

    ntile = T // 128
    scale = 1.0 / np.sqrt(DH)
    for h in range(H):
        P.wait("gpsimd", mm, mm.n)
        load_w_block(P, B, wq, dram["L0_attn_w_qkv"], 0, 1024,
                     128 * h, 128 * h + 128, gld)
        load_w_block(P, B, wk, dram["L0_attn_w_qkv"], 0, 1024,
                     1024 + 128 * h, 1024 + 128 * h + 128, gld)
        load_w_block(P, B, wv, dram["L0_attn_w_qkv"], 0, 1024,
                     2048 + 128 * h, 2048 + 128 * h + 128, gld)
        ld_done = gld.n
        for (dst, wsrc) in ((qf, wq), (kf, wk), (vf, wv)):
            for t0 in range(0, T, 512):
                j, bank, free_at = pool.take2()
                P.waitt("tensor", free_at)
                P.wait("tensor", gld, ld_done)
                if h == 0 and t0 == 0:
                    P.waitv("tensor", xn_rdy)
                mm_accum(P, mm, bank,
                         [(wsrc.ch(k), xn.ch(k, t0, t0 + 512), 128)
                          for k in range(8)])
                P.wait("vector", mm, mm.n)
                if dst is qf:
                    P.sinc("vector",
                           lambda e_, b=bank, dst=dst, t0=t0:
                           e_.tensor_scalar(dst.ch(0, t0, t0 + 512),
                                            b[0:128, 0:512], scale, None,
                                            ALU.mult), vs, 1)
                else:
                    P.sinc("vector",
                           lambda e_, b=bank, dst=dst, t0=t0:
                           e_.tensor_copy(dst.ch(0, t0, t0 + 512),
                                          b[0:128, 0:512]), vs, 1)
                pool.consumed(j, vs.n)
        qkv_done = (vs, vs.n)
        for tt in range(ntile):
            j, bank, free_at = pool.take2()
            P.waitt("tensor", free_at)
            P.waitv("tensor", qkv_done)
            P.waitv("tensor", mask_rdy)
            bb = bank
            P.inc("tensor",
                  lambda e_, b=bank, tt=tt:
                  e_.transpose(b[:, 0:128].bitcast(BF16)[:, 0:128],
                               vf.ch(0, 128 * tt, 128 * tt + 128),
                               C.eye[:, :]), mm, 1)
            P.wait("vector", mm, mm.n)
            P.sinc("vector",
                   lambda e_, b=bank, tt=tt:
                   e_.tensor_copy(vtm.ch(tt), b[:, 0:128].bitcast(BF16)[:, 0:128]),
                   vs, 1)
            pool.consumed(j, vs.n)
        for (src, dstc0, do_scale) in ((qf, 0, True), (kf, T, False)):
            for t0 in range(0, T, 512):
                j, bank, free_at = pool.take2()
                P.waitt("tensor", free_at)
                P.waitv("tensor", dirs_rdy)
                P.inc("tensor",
                      lambda e_, b=bank, src=src, t0=t0, h=h:
                      e_.matmul(b[0:1, 0:512], dirs[0:3, h:h + 1],
                                src.chp(0, 0, 3, t0, t0 + 512),
                                start=True, stop=True), mm, 1)
                P.wait("vector", mm, mm.n)
                if do_scale:
                    P.waitv("vector", hs_rdy)
                    P.sinc("vector",
                           lambda e_, b=bank, t0=t0, dstc0=dstc0, h=h:
                           e_.tensor_scalar(qp[0:1, dstc0 + t0:dstc0 + t0 + 512],
                                            b[0:1, 0:512], hsr[0:1, h:h + 1],
                                            None, ALU.mult), vs, 1)
                else:
                    P.sinc("vector",
                           lambda e_, b=bank, t0=t0, dstc0=dstc0:
                           e_.tensor_copy(qp[0:1, dstc0 + t0:dstc0 + t0 + 512],
                                          b[0:1, 0:512]), vs, 1)
                pool.consumed(j, vs.n)
        qp_rdy = (vs, vs.n)
        for qt in range(ntile):
            klen = 128 * (qt + 1)
            q0 = 128 * qt
            for k0 in range(0, klen, 512):
                kw = min(512, klen - k0)
                j, bank, free_at = pool.take2()
                P.waitt("tensor", free_at)
                P.waitv("tensor", qp_rdy)
                def smm(e_, b=bank, q0=q0, k0=k0, kw=kw, h=h):
                    e_.matmul(b[0:128, 0:kw], qf.ch(0, q0, q0 + 128),
                              kf.ch(0, k0, k0 + kw), start=True, stop=False)
                    return e_.matmul(b[0:128, 0:kw], qp[0:1, q0:q0 + 128],
                                     qp[0:1, T + k0:T + k0 + kw],
                                     start=False, stop=True)
                P.inc("tensor", smm, mm, 1)
                P.wait("vector", mm, mm.n)
                def evac(e_, b=bank, k0=k0, kw=kw, q0=q0):
                    return e_.tensor_copy(ssb[:, k0:k0 + kw], b[0:128, 0:kw])
                if k0 <= q0 < k0 + kw:
                    off = q0 - k0
                    def evac(e_, b=bank, k0=k0, kw=kw, off=off):
                        if off > 0:
                            e_.tensor_copy(ssb[:, k0:k0 + off], b[0:128, 0:off])
                        return e_.tensor_add(ssb[:, k0 + off:k0 + off + 128],
                                             b[0:128, off:off + 128],
                                             cmask[:, :])
                P.sinc("vector", evac, vs, 1)
                pool.consumed(j, vs.n)
            P.sinc("vector",
                   lambda e_, klen=klen:
                   e_.tensor_reduce(lrow[:, 0:1], ssb[:, 0:klen], AX.X,
                                    ALU.max), vs, 1)
            P.sinc("vector",
                   lambda e_: e_.tensor_scalar(lrow[:, 1:2], lrow[:, 0:1],
                                               -1.0, None, ALU.mult), vs, 1)
            P.wait("scalar", vs, vs.n)
            P.sinc("scalar",
                   lambda e_, klen=klen:
                   e_.activation(psb[:, 0:klen], ssb[:, 0:klen], AF.Exp,
                                 bias=lrow[:, 1:2], accum_out=lrow[:, 0:1]),
                   ss, 1)
            P.wait("vector", ss, ss.n)
            P.sinc("vector",
                   lambda e_: e_.reciprocal(lrow[:, 1:2], lrow[:, 0:1]), vs, 1)
            P.sinc("vector",
                   lambda e_, klen=klen:
                   e_.tensor_scalar(psb[:, 0:klen], psb[:, 0:klen],
                                    lrow[:, 1:2], None, ALU.mult), vs, 1)
            p_rdy = vs.n
            jo, bo, fo = pool.take2()
            jt, bt, ft = pool.take2()
            P.waitt("tensor", ft)
            P.waitt("tensor", fo)
            P.wait("tensor", vs, p_rdy)
            ftv = 0
            nk = klen // 128
            for kt in range(nk):
                P.wait("tensor", vs, ftv)
                P.inc("tensor",
                      lambda e_, b=bt, kt=kt:
                      e_.transpose(b[:, 0:128].bitcast(BF16)[:, 0:128],
                                   psb[:, 128 * kt:128 * kt + 128],
                                   C.eye[:, :]), mm, 1)
                P.wait("vector", mm, mm.n)
                P.sinc("vector",
                       lambda e_, b=bt, kt=kt:
                       e_.tensor_copy(
                           ssb[:, :].bitcast(BF16)[:, 128 * kt:128 * kt + 128],
                           b[:, 0:128].bitcast(BF16)[:, 0:128]),
                       vs, 1)
                ftv = vs.n
            P.wait("tensor", vs, vs.n)
            steps = [(vtm.ch(kt),
                      ssb[:, :].bitcast(BF16)[:, 128 * kt:128 * kt + 128], 128)
                     for kt in range(nk)]
            mm_accum(P, mm, bo, steps)
            P.wait("vector", mm, mm.n)
            P.sinc("vector",
                   lambda e_, b=bo, h=h, q0=q0:
                   e_.tensor_copy(atn.ch(h, q0, q0 + 128), b[0:128, 0:128]),
                   vs, 1)
            pool.consumed(jo, vs.n)
            pool.consumed(jt, vs.n)
    atn_rdy = (vs, vs.n)
    for t0 in range(0, T, 512):
        for oc in range(8):
            j, bank, free_at = pool.take2()
            P.waitt("tensor", free_at)
            if t0 == 0 and oc == 0:
                P.waitv("tensor", atn_rdy)
                P.wait("tensor", gld, gld.n)
            mm_accum(P, mm, bank,
                     [(wo.ch(k)[:, 128 * oc:128 * (oc + 1)],
                       atn.ch(k, t0, t0 + 512), 128) for k in range(8)])
            P.wait("vector", mm, mm.n)
            P.sinc("vector",
                   lambda e_, b=bank, oc=oc, t0=t0:
                   e_.tensor_add(x.ch(oc, t0, t0 + 512), x.ch(oc, t0, t0 + 512),
                                 b[0:128, 0:512]), vs, 1)
            pool.consumed(j, vs.n)
    return (vs, vs.n)


def build_l0attn_launch(T=2048):
    names = ["L0_attn_w_qkv", "L0_attn_w_out", "L0_attn_head_scales"]
    def body(env):
        return emit_l0attn(env["P"], env["B"], env["C"], env["L"], env["pool"],
                           env["x"], env["xn"], env["T"], env["dram"],
                           env["x_ready"])
    return make_launch(names, body, T, npool=3)

def emit_ln_stream(P, B, C, L, xin_dram, xn, T, xld, xtile, extra_wait=()):
    for ti, t0 in enumerate(range(0, T, 512)):
        t1 = t0 + 512
        P.wait("sync", L.ts, L.ts.n)
        P.wait("sync", L.vs, L.vs.n)
        for sv in extra_wait:
            P.waitv("sync", sv)
        for k in range(8):
            P.dma("sync", xtile.ch(k), xin_dram[128 * k:128 * (k + 1), t0:t1],
                  xld)
        class _V:
            def ch(self, k, a, b):
                return xtile.ch(k, a, b)
        emit_ln(L, _V(), xn, 512, (xld, xld.n), t_base=t0)
    return (L.vs, L.vs.n)

def emit_l1attn(P, B, C, L, pool, x_src_dram, x_dst_dram, T, dram):
    H, DH = 16, 64
    HT = T // 2
    mm = P.sem("l1a_mm")
    vs = pool.sem
    ss = P.sem("l1a_ss")
    gld = P.sem("l1a_gld")
    bld = P.sem("l1a_bld")
    xld = P.sem("l1a_xld")
    st2 = P.sem("l1a_st2")
    gcs = P.sem("l1a_gcs")

    xtile = B.fm("l1a_xtile", 512, 8, F32)
    z0 = B.fm("l1a_z0", HT, 8, BF16)
    xn1 = B.fm("l1a_xn1", HT, 8, BF16)
    z1 = B.fm("l1a_z1", HT, 8, BF16)
    atn = B.fm("l1a_atn", HT, 8, BF16)
    gfm = B.fm("l1a_g", HT, 8, BF16)
    drive = B.sb("l1a_drive", [128, HT], F32)
    states = B.sb("l1a_states", [128, HT], BF16)
    scarry = B.sb("l1a_scarry", [128, 1], BF16)
    acol = B.sb("l1a_acol", [128, 1], F32)
    xh = B.fm("l1a_xh", 256, 8, BF16)
    qf = B.sb("l1a_qf", [64, HT], BF16)
    kf = B.sb("l1a_kf", [64, HT + 256], BF16)
    vf = B.sb("l1a_vf", [64, HT + 256], BF16)
    vtm = B.sb("l1a_vtm", [128, ((HT + 256) // 128) * 64], BF16)
    ssb = B.sb("l1a_ssb", [128, 384], F32)
    psb = B.sb("l1a_psb", [128, 384], BF16)
    lrow = B.sb("l1a_lrow", [128, 2], F32)
    yt = B.sb("l1a_yt", [128, 512], BF16)
    t1s = B.sb("l1a_t1s", [128, 512], F32)
    wbig = B.fm("l1a_wbig", 1152, 8, BF16)
    wqh = B.fm("l1a_wqh", 64, 8, BF16)
    wkh = B.fm("l1a_wkh", 64, 8, BF16)
    wvh = B.fm("l1a_wvh", 64, 8, BF16)
    w2b = B.fm("l1a_w2b", 128, 32, BF16)
    cwn = B.fm("l1a_cw", 1024, 1, BF16)
    bwn = B.fm("l1a_bw", 128, 8, BF16)
    cmask = B.sb("l1a_cmask", [128, 128], BF16)
    umask = B.sb("l1a_umask", [128, 128], BF16)

    g1 = load_bias_cols(P, B, "l1a_g1", dram["L1_attn_ln1_g"], 1024, bld)
    b1 = load_bias_cols(P, B, "l1a_b1", dram["L1_attn_ln1_b"], 1024, bld)
    g2 = load_bias_cols(P, B, "l1a_g2", dram["L1_attn_ln2_g"], 1024, bld)
    b2 = load_bias_cols(P, B, "l1a_b2", dram["L1_attn_ln2_b"], 1024, bld)
    bO = load_bias_cols(P, B, "l1a_bO", dram["L1_attn_b_O"], 1024, bld)
    bug = load_bias_cols(P, B, "l1a_bug", dram["L1_attn_b_ug"], 1152, bld)
    mb1 = load_bias_cols(P, B, "l1a_mb1", dram["L1_attn_mlp_b1"], 4096, bld)
    mb2 = load_bias_cols(P, B, "l1a_mb2", dram["L1_attn_mlp_b2"], 1024, bld)
    load_bias_cols(P, B, None, dram["L1_attn_A"], 128, bld, tile=acol)
    bias_rdy = (bld, bld.n)

    P.inc("gpsimd", lambda e_: e_.memset(cmask[:], 0.0), gcs, 1)
    P.inc("gpsimd", lambda e_: e_.memset(umask[:], 0.0), gcs, 1)
    P.wait("gpsimd", gcs, gcs.n)
    P.inc("gpsimd",
          lambda e_: e_.affine_select(cmask[:], cmask[:], pattern=[[-1, 128]],
                                      compare_op=ALU.is_ge, fill=-1e30,
                                      base=0, channel_multiplier=1), gcs, 1)
    P.wait("gpsimd", gcs, gcs.n)
    P.inc("gpsimd",
          lambda e_: e_.affine_select(umask[:], umask[:], pattern=[[1, 128]],
                                      compare_op=ALU.is_ge, fill=-1e30,
                                      base=-1, channel_multiplier=-1), gcs, 1)
    mask_rdy = (gcs, gcs.n)

    bwn_, _ = load_w_nat(P, B, "l1a_bwn", dram["L1_attn_B_w"], 1024, 128,
                         gld, out_fm=bwn)
    cwn_, _ = load_w_nat(P, B, "l1a_cwn", dram["L1_attn_C_w"], 128, 1024,
                         gld, out_fm=cwn)


    ntile = HT // 128
    scale = 1.0 / np.sqrt(DH)
    for hf in range(2):
        h0 = HT * hf
        halo = 256 if hf == 1 else 0
        z0_rdy = emit_ln_stream(P, B, C, L, x_src_dram[:, h0:h0 + HT], z0, HT,
                                xld, xtile, extra_wait=[(st2, st2.n)])
        P.waitv("vector", bias_rdy)
        xn1_rdy = emit_ln(L, z0, xn1, HT, z0_rdy, gain=(g1, b1), in_bf=True)
        if hf == 0:
            for k in range(8):
                P.waitv("vector", xn1_rdy)
                P.sinc("vector",
                       lambda e_, k=k:
                       e_.tensor_copy(xh.ch(k), xn1.ch(k, HT - 256, HT)),
                       vs, 1)
            xh_rdy = (vs, vs.n)
        P.wait("gpsimd", mm, mm.n)
        wug, _ = load_w_nat(P, B, "l1a_wug", dram["L1_attn_W_ug"], 1024, 1152,
                            gld, out_fm=wbig)
        w_rdy = (gld, gld.n)
        for t0 in range(0, HT, 512):
            t1 = t0 + 512
            for oc in range(9):
                j, bank, free_at = pool.take2()
                P.waitt("tensor", free_at)
                if t0 == 0 and oc == 0:
                    P.waitv("tensor", w_rdy)
                    P.waitv("tensor", xn1_rdy)
                steps = [(wbig.chp(k, 0, 128, 128 * oc, 128 * (oc + 1)),
                          xn1.ch(k, t0, t1), 128) for k in range(8)]
                if oc == 0:
                    steps += [(bwn.chp(k, 0, 128, 0, 128), xn1.ch(k, t0, t1),
                               128) for k in range(8)]
                mm_accum(P, mm, bank, steps)
                if oc == 0:
                    P.wait("vector", mm, mm.n)
                    P.waitv("vector", bias_rdy)
                    P.sinc("vector",
                           lambda e_, b=bank, t0=t0, t1=t1:
                           e_.tensor_scalar(drive[:, t0:t1], b[0:128, 0:512],
                                            bug[:, 0:1], None, ALU.add), vs, 1)
                    pool.consumed(j, vs.n)
                else:
                    P.wait("scalar", mm, mm.n)
                    P.wait("scalar", vs, vs.n)
                    P.sinc("scalar",
                           lambda e_, b=bank, oc=oc, t0=t0, t1=t1:
                           e_.activation(gfm.ch(oc - 1, t0, t1),
                                         b[0:128, 0:512], AF.Sigmoid,
                                         bias=bug[:, oc:oc + 1]), ss, 1)
                    pool.consumed(j, ss.n, ss)
        P.wait("vector", ss, ss.n)
        P.sinc("vector", lambda e_: e_.engine_nop(), vs, 1)
        init = 0.0 if hf == 0 else scarry[:, 0:1]
        P.sinc("vector",
               lambda e_, init=init:
               e_.tensor_tensor_scan(states[:, :],
                                     acol.broadcast_to([128, HT]),
                                     drive[:, :], init, ALU.mult, ALU.add),
               vs, 1)
        if hf == 0:
            P.sinc("vector",
                   lambda e_: e_.tensor_copy(scarry[:, 0:1],
                                             states[:, HT - 1:HT]), vs, 1)
        scan_rdy = (vs, vs.n)
        for h in range(H):
            P.wait("gpsimd", mm, mm.n)
            load_w_block(P, B, wqh, dram["L1_attn_W_qkv"], 0, 1024,
                         64 * h, 64 * h + 64, gld)
            load_w_block(P, B, wkh, dram["L1_attn_W_qkv"], 0, 1024,
                         1024 + 64 * h, 1024 + 64 * h + 64, gld)
            load_w_block(P, B, wvh, dram["L1_attn_W_qkv"], 0, 1024,
                         2048 + 64 * h, 2048 + 64 * h + 64, gld)
            ld_done = gld.n
            base_tiles = [(t0, 512) for t0 in range(0, HT, 512)]
            for (dst, wsrc, dosc, tiles) in (
                    (qf, wqh, True, base_tiles),
                    (kf, wkh, False, ([(-256, 256)] if halo else []) + base_tiles),
                    (vf, wvh, False, ([(-256, 256)] if halo else []) + base_tiles)):
                for (t0, tw) in tiles:
                    j, bank, free_at = pool.take2()
                    P.waitt("tensor", free_at)
                    P.wait("tensor", gld, ld_done)
                    if t0 < 0:
                        P.waitv("tensor", xh_rdy)
                        steps = [(wsrc.chp(k, 0, 128, 0, 64),
                                  xh.ch(k, t0 + 256, t0 + 256 + tw), 64)
                                 for k in range(8)]
                    else:
                        steps = [(wsrc.chp(k, 0, 128, 0, 64),
                                  xn1.ch(k, t0, t0 + tw), 64)
                                 for k in range(8)]
                    mm_accum(P, mm, bank, steps)
                    P.wait("vector", mm, mm.n)
                    col0 = t0 + (halo if dst is not qf else 0)
                    if dosc:
                        P.sinc("vector",
                               lambda e_, b=bank, dst=dst, col0=col0, tw=tw:
                               e_.tensor_scalar(dst[0:64, col0:col0 + tw],
                                                b[0:64, 0:tw], scale, None,
                                                ALU.mult), vs, 1)
                    else:
                        P.sinc("vector",
                               lambda e_, b=bank, dst=dst, col0=col0, tw=tw:
                               e_.tensor_copy(dst[0:64, col0:col0 + tw],
                                              b[0:64, 0:tw]), vs, 1)
                    pool.consumed(j, vs.n)
            nkt = (HT + halo) // 128
            P.wait("tensor", vs, vs.n)
            for tt in range(nkt):
                j, bank, free_at = pool.take2()
                P.waitt("tensor", free_at)
                P.waitv("tensor", mask_rdy)
                P.inc("tensor",
                      lambda e_, b=bank, tt=tt:
                      e_.transpose(b[:, 0:64].bitcast(BF16)[:, 0:64],
                                   vf[0:64, 128 * tt:128 * tt + 128],
                                   C.eye[0:64, 0:64]), mm, 1)
                P.wait("vector", mm, mm.n)
                P.sinc("vector",
                       lambda e_, b=bank, tt=tt:
                       e_.tensor_copy(vtm[:, 64 * tt:64 * tt + 64],
                                      b[:, 0:64].bitcast(BF16)[:, 0:64]),
                       vs, 1)
                pool.consumed(j, vs.n)
            for qt in range(ntile):
                qtg = ntile * hf + qt
                q0 = 128 * qt
                if qtg == 0:
                    blocks = [(qtg, "c")]
                elif qtg == 1:
                    blocks = [(qtg - 1, "f"), (qtg, "c")]
                else:
                    blocks = [(qtg - 2, "u"), (qtg - 1, "f"), (qtg, "c")]
                nb = len(blocks)
                kloc0 = h0 - halo
                j, bank, free_at = pool.take2()
                P.waitt("tensor", free_at)
                for bi, (ktg, kind) in enumerate(blocks):
                    kc = 128 * ktg - kloc0
                    P.inc("tensor",
                          lambda e_, b=bank, bi=bi, kc=kc, q0=q0:
                          e_.matmul(b[0:128, 128 * bi:128 * bi + 128],
                                    qf[0:64, q0:q0 + 128],
                                    kf[0:64, kc:kc + 128],
                                    start=True, stop=True), mm, 1)
                P.wait("vector", mm, mm.n)
                for bi, (ktg, kind) in enumerate(blocks):
                    if kind == "f":
                        P.sinc("vector",
                               lambda e_, b=bank, bi=bi:
                               e_.tensor_copy(ssb[:, 128 * bi:128 * bi + 128],
                                              b[0:128, 128 * bi:128 * bi + 128]),
                               vs, 1)
                    else:
                        msk = cmask if kind == "c" else umask
                        P.sinc("vector",
                               lambda e_, b=bank, bi=bi, msk=msk:
                               e_.tensor_add(ssb[:, 128 * bi:128 * bi + 128],
                                             b[0:128, 128 * bi:128 * bi + 128],
                                             msk[:, :]), vs, 1)
                pool.consumed(j, vs.n)
                kw = 128 * nb
                P.sinc("vector",
                       lambda e_, kw=kw:
                       e_.tensor_reduce(lrow[:, 0:1], ssb[:, 0:kw], AX.X,
                                        ALU.max), vs, 1)
                P.sinc("vector",
                       lambda e_: e_.tensor_scalar(lrow[:, 1:2], lrow[:, 0:1],
                                                   -1.0, None, ALU.mult),
                       vs, 1)
                P.wait("scalar", vs, vs.n)
                P.sinc("scalar",
                       lambda e_, kw=kw:
                       e_.activation(psb[:, 0:kw], ssb[:, 0:kw], AF.Exp,
                                     bias=lrow[:, 1:2],
                                     accum_out=lrow[:, 0:1]), ss, 1)
                P.wait("vector", ss, ss.n)
                P.sinc("vector",
                       lambda e_: e_.reciprocal(lrow[:, 1:2], lrow[:, 0:1]),
                       vs, 1)
                P.sinc("vector",
                       lambda e_, kw=kw:
                       e_.tensor_scalar(psb[:, 0:kw], psb[:, 0:kw],
                                        lrow[:, 1:2], None, ALU.mult), vs, 1)
                jt, bt, ft = pool.take2()
                P.waitt("tensor", ft)
                ftv = 0
                for bi in range(nb):
                    P.wait("tensor", vs, max(ftv, vs.n))
                    P.inc("tensor",
                          lambda e_, b=bt, bi=bi:
                          e_.transpose(b[:, 0:128].bitcast(BF16)[:, 0:128],
                                       psb[:, 128 * bi:128 * bi + 128],
                                       C.eye[:, :]), mm, 1)
                    P.wait("vector", mm, mm.n)
                    P.sinc("vector",
                           lambda e_, b=bt, bi=bi:
                           e_.tensor_copy(
                               ssb[:, :].bitcast(BF16)[:, 128 * bi:128 * bi + 128],
                               b[:, 0:128].bitcast(BF16)[:, 0:128]), vs, 1)
                    ftv = vs.n
                jo, bo, fo = pool.take2()
                P.waitt("tensor", fo)
                P.wait("tensor", vs, vs.n)
                steps = []
                for bi, (ktg, kind) in enumerate(blocks):
                    kc = 128 * ktg - kloc0
                    steps.append(
                        (vtm[:, (kc // 128) * 64:(kc // 128) * 64 + 64],
                         ssb[:, :].bitcast(BF16)[:, 128 * bi:128 * bi + 128],
                         64))
                mm_accum(P, mm, bo, steps)
                P.wait("vector", mm, mm.n)
                ch, po = h // 2, 64 * (h % 2)
                P.sinc("vector",
                       lambda e_, b=bo, ch=ch, po=po, q0=q0:
                       e_.tensor_copy(atn.chp(ch, po, po + 64, q0, q0 + 128),
                                      b[0:64, 0:128]), vs, 1)
                pool.consumed(jo, vs.n)
                pool.consumed(jt, vs.n)
        atn_rdy = (vs, vs.n)
        P.wait("gpsimd", mm, mm.n)
        wo, _ = load_w_nat(P, B, "l1a_wo", dram["L1_attn_W_O"], 1024, 1024,
                           gld, out_fm=wbig)
        wo_rdy = (gld, gld.n)
        for t0 in range(0, HT, 512):
            t1 = t0 + 512
            for oc in range(8):
                j2, b2k, f2 = pool.take2()
                P.waitt("tensor", f2)
                if t0 == 0 and oc == 0:
                    P.waitv("tensor", scan_rdy)
                    P.waitv("tensor", wo_rdy)
                    P.waitv("tensor", atn_rdy)
                P.inc("tensor",
                      lambda e_, b=b2k, oc=oc, t0=t0, t1=t1:
                      e_.matmul(b[0:128, 0:512],
                                cwn.chp(0, 0, 128, 128 * oc, 128 * (oc + 1)),
                                states[:, t0:t1], start=True, stop=True),
                      mm, 1)
                P.wait("vector", mm, mm.n)
                P.sinc("vector",
                       lambda e_, b=b2k: e_.tensor_copy(yt[:, 0:512],
                                                        b[0:128, 0:512]),
                       vs, 1)
                pool.consumed(j2, vs.n)
                j, bank, free_at = pool.take2()
                P.waitt("tensor", free_at)
                mm_accum(P, mm, bank,
                         [(wbig.chp(k, 0, 128, 128 * oc, 128 * (oc + 1)),
                           atn.ch(k, t0, t1), 128) for k in range(8)])
                P.wait("vector", mm, mm.n)
                P.sinc("vector",
                       lambda e_, b=bank, oc=oc:
                       e_.scalar_tensor_tensor(t1s[:, 0:512], b[0:128, 0:512],
                                               bO[:, oc:oc + 1], yt[:, 0:512],
                                               ALU.add, ALU.subtract), vs, 1)
                pool.consumed(j, vs.n)
                P.sinc("vector",
                       lambda e_, oc=oc, t0=t0, t1=t1:
                       e_.tensor_mul(t1s[:, 0:512], t1s[:, 0:512],
                                     gfm.ch(oc, t0, t1)), vs, 1)
                P.sinc("vector",
                       lambda e_, oc=oc, t0=t0, t1=t1:
                       e_.tensor_add(z1.ch(oc, t0, t1), z0.ch(oc, t0, t1),
                                     yt[:, 0:512]), vs, 1)
                P.sinc("vector",
                       lambda e_, oc=oc, t0=t0, t1=t1:
                       e_.tensor_add(z1.ch(oc, t0, t1), z1.ch(oc, t0, t1),
                                     t1s[:, 0:512]), vs, 1)
        z1_rdy = (vs, vs.n)
        xn2 = xn1
        if HT <= 512:
            if not hasattr(P, "_l1a_mact_dbg"):
                P._l1a_mact_dbg = B.fm("l1a_mactd", 256, 32, BF16)
            mact = P._l1a_mact_dbg
        else:
            mact = FM(atn.t, 256, 32)
        P.waitv("vector", bias_rdy)
        xn2_rdy = emit_ln(L, z1, xn2, HT, z1_rdy, gain=(g2, b2), in_bf=True)
        for t0 in range(0, HT, 256):
            t1 = t0 + 256
            for jb in range(8):
                P.wait("gpsimd", mm, mm.n)
                load_w_block(P, B, wbig, dram["L1_attn_mlp_W1"], 0, 1024,
                             512 * jb, 512 * jb + 512, gld)
                ld_done = gld.n
                for fc in range(4):
                    jch = 4 * jb + fc
                    j, bank, free_at = pool.take2()
                    P.waitt("tensor", free_at)
                    P.wait("tensor", gld, ld_done)
                    if jb == 0 and fc == 0:
                        P.waitv("tensor", xn2_rdy)
                    mm_accum(P, mm, bank,
                             [(wbig.chp(k, 0, 128, 128 * fc, 128 * (fc + 1)),
                               xn2.ch(k, t0, t1), 128) for k in range(8)])
                    P.wait("scalar", mm, mm.n)
                    P.wait("scalar", vs, vs.n)
                    P.sinc("scalar",
                           lambda e_, b=bank, jch=jch:
                           e_.activation(mact.ch(jch), b[0:128, 0:256],
                                         AF.Gelu, bias=mb1[:, jch:jch + 1]),
                           ss, 1)
                    P.wait("vector", ss, ss.n)
                    P.sinc("vector", lambda e_: e_.engine_nop(), vs, 1)
                    pool.consumed(j, vs.n)
            for oc in range(8):
                P.wait("gpsimd", mm, mm.n)
                load_w_block(P, B, w2b, dram["L1_attn_mlp_W2"], 0, 4096,
                             128 * oc, 128 * (oc + 1), gld)
                ld_done = gld.n
                j, bank, free_at = pool.take2()
                P.waitt("tensor", free_at)
                P.wait("tensor", gld, ld_done)
                P.wait("tensor", vs, vs.n)
                mm_accum(P, mm, bank,
                         [(w2b.chp(k, 0, 128, 0, 128), mact.ch(k), 128)
                          for k in range(32)])
                P.wait("vector", mm, mm.n)
                P.sinc("vector",
                       lambda e_, b=bank, oc=oc, t0=t0, t1=t1:
                       e_.scalar_tensor_tensor(z1.ch(oc, t0, t1),
                                               b[0:128, 0:256],
                                               mb2[:, oc:oc + 1],
                                               z1.ch(oc, t0, t1),
                                               ALU.add, ALU.add), vs, 1)
                pool.consumed(j, vs.n)
        z2_rdy = (vs, vs.n)
        for oc in range(8):
            for t0 in range(0, HT, 512):
                t1 = t0 + 512
                P.wait("sync", vs, vs.n)
                P.wait("sync", st2, st2.n)
                P.dma("sync", xtile.ch(0, 0, 512),
                      x_src_dram[128 * oc:128 * (oc + 1), h0 + t0:h0 + t1],
                      xld)
                P.wait("vector", xld, xld.n)
                P.waitv("vector", z2_rdy)
                P.sinc("vector",
                       lambda e_, oc=oc, t0=t0, t1=t1:
                       e_.tensor_add(xtile.ch(0, 0, 512), xtile.ch(0, 0, 512),
                                     z1.ch(oc, t0, t1)), vs, 1)
                P.wait("sync", vs, vs.n)
                P.dma("sync",
                      x_dst_dram[128 * oc:128 * (oc + 1), h0 + t0:h0 + t1],
                      xtile.ch(0, 0, 512), st2)
    P.wait("sync", st2, st2.n)
    return (st2, st2.n)


def build_l1attn_launch(T=2048):
    from contextlib import ExitStack
    nc = bass.Bass("TRN2", target_bir_lowering=False)
    XIN = nc.dram_tensor("XIN", [1024, T], F32, kind="ExternalInput").ap()
    XOUT = nc.dram_tensor("XOUT", [1024, T], F32, kind="ExternalOutput").ap()
    names = [n for n in SHAPES if n.startswith("L1_attn")]
    dram = {nm: nc.dram_tensor(nm, SHAPES[nm], F32, kind="ExternalInput").ap()
            for nm in names}
    ctx = ExitStack()
    with ctx:
        B = Bufs(nc, ctx)
        P = Prog(nc, ctx)
        C = Consts(P, B)
        p_row = B.ps("p_row", [1, 512], F32)
        p_bm = B.ps("p_bm", [128, 512], F32)
        p_br = B.ps("p_br", [128, 512], F32)
        ev = P.sem("ev")
        pk = [B.ps(f"pk{i}", [128, 512], F32) for i in range(5)]
        pool = Pool(pk, ev)
        L = LNCtx(P, B, C, p_row, p_bm, p_br)
        emit_l1attn(P, B, C, L, pool, XIN, XOUT, T, dram)
        P.emit()
    return nc

def emit_l2attn(P, B, C, L, pool, wt, x, xn, T, dram, x_ready):
    CH, NCH = 64, T // 64
    mm = P.sem("l2a_mm")
    vs = pool.sem
    ss = P.sem("l2a_ss")
    gld = P.sem("l2a_gld")
    bld = P.sem("l2a_bld")
    esb = P.sem("l2a_esb")
    xn_rdy = emit_ln(L, x, xn, T, x_ready)

    M0T, _ = wt.load_T("l2a_m0", dram["L2_attn_M_init"], 1024, 1024)
    vg1T, _ = wt.load_T("l2a_vg1", dram["L2_attn_vg_w1"], 1024, 1024)
    vg2T, vgT_rdy = wt.load_T("l2a_vg2", dram["L2_attn_vg_w2"], 1024, 1024)
    etw = load_bias_cols(P, B, "l2a_etw", dram["L2_attn_eta_w"], 1024, gld,
                         eng="gpsimd", dt=BF16)
    alw = load_bias_cols(P, B, "l2a_alw", dram["L2_attn_alpha_w"], 1024, gld,
                         eng="gpsimd", dt=BF16)
    gaw = load_bias_cols(P, B, "l2a_gaw", dram["L2_attn_gate_w"], 1024, gld,
                         eng="gpsimd", dt=BF16)
    wv_rdy = (gld, gld.n)
    ebias = B.sb("l2a_eb", [1, 3], F32)
    P.dma("scalar", ebias[0:1, 0:1],
          dram["L2_attn_eta_b"].rearrange("(a b) -> a b", a=1), bld)
    P.dma("scalar", ebias[0:1, 1:2],
          dram["L2_attn_alpha_b"].rearrange("(a b) -> a b", a=1), bld)
    P.dma("scalar", ebias[0:1, 2:3],
          dram["L2_attn_gate_b"].rearrange("(a b) -> a b", a=1), bld)
    eb_rdy = (bld, bld.n)

    OB = B.fm("l2a_ob", T, 8, BF16)
    KA = B.fm("l2a_ka", NCH, 8, F32)
    KAb = B.fm("l2a_kab", NCH, 8, BF16)
    Etm = B.sb("l2a_etm", [32, 1024], BF16)
    M0K = B.fm("l2a_m0k", NCH, 8, F32)
    rows = B.sb("l2a_rows", [1, T], BF16)
    crow = B.sb("l2a_crow", [1, 64], F32)
    kkrow = B.sb("l2a_kk", [1, NCH], F32)
    b0col_t = B.sb("l2a_b0c", [NCH, 1], F32)
    P.sinc("vector", lambda e_: e_.memset(b0col_t[:], 0.0), vs, 1)
    Gsb = B.sb("l2a_g", [32, 32], F32)
    gsc = B.sb("l2a_gsc", [128, 1024], BF16)
    scal = B.sb("l2a_scal", [1, 16], F32)
    gb = B.sb("l2a_gb", [128, 1], F32)
    bcol = B.sb("l2a_bcol", [32, 1], F32)
    bcolb = B.sb("l2a_bcolb", [32, 2], BF16)
    kcb = B.sb("l2a_kcb", [32, 64], BF16)
    outj = B.fm("l2a_outj", 64, 8, BF16)
    h1j = B.fm("l2a_h1j", 64, 8, BF16)
    vt = B.sb("l2a_vt", [128, 8 * 3], F32)
    errb = B.sb("l2a_errb", [128, 16], BF16)
    gaterow = B.sb("l2a_gate", [1, NCH], F32)
    etarow = B.sb("l2a_eta", [1, NCH], F32)
    alpharow = B.sb("l2a_alpha", [1, NCH], F32)
    c1row = B.sb("l2a_c1", [1, NCH], F32)
    c2row = B.sb("l2a_c2", [1, NCH], F32)

    for t0 in range(0, T, 512):
        for oc in range(8):
            j, bank, fa = pool.take2()
            P.waitt("tensor", fa)
            if t0 == 0 and oc == 0:
                P.waitv("tensor", vgT_rdy)
                P.waitv("tensor", xn_rdy)
            mm_accum(P, mm, bank,
                     [(M0T.chp(k, 0, 128, 128 * oc, 128 * (oc + 1)),
                       xn.ch(k, t0, t0 + 512), 128) for k in range(8)])
            P.wait("vector", mm, mm.n)
            P.sinc("vector",
                   lambda e_, b=bank, oc=oc, t0=t0:
                   e_.tensor_copy(OB.ch(oc, t0, t0 + 512), b[0:128, 0:512]),
                   vs, 1)
            pool.consumed(j, vs.n)

    for (wcol, brow, dst, scl, off) in (
            (etw, ebias[0:1, 0:1], etarow, 0.2 / CH, 0.0),
            (alw, ebias[0:1, 1:2], alpharow, 0.5 / CH, 0.5)):
        for t0 in range(0, T, 512):
            j, bank, fa = pool.take2()
            P.waitt("tensor", fa)
            if t0 == 0:
                P.waitv("tensor", wv_rdy)
            mm_accum(P, mm, bank,
                     [(wcol[:, k:k + 1], xn.ch(k, t0, t0 + 512), 1)
                      for k in range(8)])
            P.wait("scalar", mm, mm.n)
            P.waitv("scalar", eb_rdy)
            P.wait("scalar", vs, vs.n)
            P.sinc("scalar",
                   lambda e_, b=bank, t0=t0, brow=brow:
                   e_.activation(rows[0:1, t0:t0 + 512], b[0:1, 0:512],
                                 AF.Sigmoid, bias=brow), ss, 1)
            P.wait("vector", ss, ss.n)
            P.sinc("vector", lambda e_: e_.engine_nop(), vs, 1)
            pool.consumed(j, vs.n)
        P.sinc("vector",
               lambda e_, dst=dst:
               e_.tensor_reduce(dst[0:1, 0:NCH],
                                rows[0:1, 0:T].rearrange("a (c s) -> a c s",
                                                         s=CH),
                                AX.X, ALU.add), vs, 1)
        P.sinc("vector",
               lambda e_, dst=dst, scl=scl, off=off:
               e_.tensor_scalar(dst[0:1, 0:NCH], dst[0:1, 0:NCH], scl, off,
                                ALU.mult, ALU.add), vs, 1)
    for t0 in range(0, T, 512):
        j, bank, fa = pool.take2()
        P.waitt("tensor", fa)
        for k in range(8):
            P.wait("scalar", L.ts, L.ts.n)
            P.wait("scalar", ss, ss.n)
            P.wait("scalar", mm, mm.n)
            P.sinc("scalar",
                   lambda e_, k=k, t0=t0:
                   e_.activation(gsc[:, 0:512], xn.ch(k, t0, t0 + 512),
                                 AF.Square), ss, 1)
            P.wait("tensor", ss, ss.n)
            P.inc("tensor",
                  lambda e_, b=bank, s=(k == 0), z=(k == 7):
                  e_.matmul(b[0:1, 0:512], C.ones_b[:, :], gsc[:, 0:512],
                            start=s, stop=z), mm, 1)
        P.wait("vector", mm, mm.n)
        P.sinc("vector",
               lambda e_, b=bank, t0=t0:
               e_.tensor_copy(rows[0:1, t0:t0 + 512],
                              b[0:1, 0:512]), vs, 1)
        pool.consumed(j, vs.n)
    P.wait("scalar", vs, vs.n)
    P.sinc("scalar",
           lambda e_: e_.activation(rows[0:1, 0:T], rows[0:1, 0:T],
                                    AF.Sqrt), ss, 1)
    P.wait("vector", ss, ss.n)
    P.sinc("vector",
           lambda e_: e_.tensor_scalar(rows[0:1, 0:T], rows[0:1, 0:T],
                                       1e-5, None, ALU.max), vs, 1)
    P.sinc("vector",
           lambda e_: e_.tensor_scalar(rows[0:1, 0:T], rows[0:1, 0:T],
                                       float(CH), None, ALU.mult), vs, 1)
    for t0 in range(0, T, 512):
        P.sinc("vector",
               lambda e_, t0=t0: e_.tensor_copy(L.apl[0:1, 0:512],
                                                rows[0:1, t0:t0 + 512]), vs, 1)
        P.sinc("vector",
               lambda e_: e_.reciprocal(L.apl[0:1, 0:512], L.apl[0:1, 0:512]),
               vs, 1)
        P.sinc("vector",
               lambda e_, t0=t0: e_.tensor_copy(rows[0:1, t0:t0 + 512],
                                                L.apl[0:1, 0:512]), vs, 1)
    rn_rdy = (vs, vs.n)
    for t0 in range(0, T, 512):
        j, bank, fa = pool.take2()
        P.waitt("tensor", fa)
        P.waitv("tensor", rn_rdy)
        P.inc("tensor",
              lambda e_, b=bank, t0=t0:
              e_.matmul(b[0:128, 0:512], C.ones_row_b[:, :],
                        rows[0:1, t0:t0 + 512], start=True, stop=True),
              mm, 1)
        P.wait("vector", mm, mm.n)
        for k in range(8):
            P.sinc("vector",
                   lambda e_, b=bank, k=k, t0=t0:
                   e_.tensor_mul(gsc[:, 0:512], xn.ch(k, t0, t0 + 512),
                                 b[0:128, 0:512]), vs, 1)
            P.sinc("vector",
                   lambda e_, k=k, t0=t0:
                   e_.tensor_reduce(KA.chp(k, 0, 128, t0 // 64, t0 // 64 + 8),
                                    gsc[:, 0:512].rearrange(
                                        "p (c s) -> p c s", s=CH),
                                    AX.X, ALU.add), vs, 1)
        pool.consumed(j, vs.n)
    for k in range(8):
        P.sinc("vector",
               lambda e_, k=k: e_.tensor_copy(KAb.ch(k), KA.ch(k)), vs, 1)
    ka_rdy = (vs, vs.n)
    j, bank, fa = pool.take2()
    P.waitt("tensor", fa)
    P.waitv("tensor", ka_rdy)
    mm_accum(P, mm, bank, [(KAb.ch(k), KAb.ch(k), NCH) for k in range(8)])
    P.wait("vector", mm, mm.n)
    P.sinc("vector", lambda e_, b=bank: e_.tensor_copy(Gsb[:, :],
                                                       b[0:NCH, 0:NCH]), vs, 1)
    pool.consumed(j, vs.n)
    P.sinc("vector",
           lambda e_: e_.tensor_mul(gsc[0:32, 0:64].bitcast(F32),
                                    Gsb[:, :], C.eye_f[0:32, 0:32]), vs, 1)
    P.sinc("vector",
           lambda e_: e_.tensor_reduce(bcol[:, 0:1],
                                       gsc[0:32, 0:64].bitcast(F32), AX.X,
                                       ALU.add), vs, 1)
    j, bank, fa = pool.take2()
    P.waitt("tensor", fa)
    P.wait("tensor", vs, vs.n)
    P.inc("tensor",
          lambda e_, b=bank: e_.transpose(b[0:1, 0:32], bcol[:, 0:1],
                                          C.eye_f[0:32, 0:32]), mm, 1)
    P.wait("vector", mm, mm.n)
    P.sinc("vector", lambda e_, b=bank: e_.tensor_copy(kkrow[0:1, 0:NCH],
                                                       b[0:1, 0:32]), vs, 1)
    pool.consumed(j, vs.n)
    for oc in range(8):
        j, bank, fa = pool.take2()
        P.waitt("tensor", fa)
        mm_accum(P, mm, bank,
                 [(M0T.chp(k, 0, 128, 128 * oc, 128 * (oc + 1)), KAb.ch(k),
                   128) for k in range(8)])
        P.wait("vector", mm, mm.n)
        P.sinc("vector",
               lambda e_, b=bank, oc=oc:
               e_.tensor_copy(M0K.ch(oc), b[0:128, 0:NCH]), vs, 1)
        pool.consumed(j, vs.n)
    j, bank, fa = pool.take2()
    P.waitt("tensor", fa)
    mm_accum(P, mm, bank, [(KAb.ch(k), gaw[:, k:k + 1], NCH)
                           for k in range(8)])
    P.wait("vector", mm, mm.n)
    P.sinc("vector", lambda e_, b=bank: e_.tensor_copy(bcol[:, 0:1],
                                                       b[0:NCH, 0:1]), vs, 1)
    j2, bank2, fa2 = pool.take2()
    P.waitt("tensor", fa2)
    P.wait("tensor", vs, vs.n)
    P.inc("tensor",
          lambda e_, b=bank2: e_.transpose(b[0:1, 0:32], bcol[:, 0:1],
                                           C.eye_f[0:32, 0:32]), mm, 1)
    P.wait("scalar", mm, mm.n)
    P.wait("scalar", vs, vs.n)
    P.sinc("scalar",
           lambda e_, b=bank2:
           e_.activation(gaterow[0:1, 0:NCH], b[0:1, 0:32], AF.Sigmoid,
                         bias=ebias[0:1, 2:3]), ss, 1)
    P.wait("vector", ss, ss.n)
    P.sinc("vector", lambda e_: e_.engine_nop(), vs, 1)
    pool.consumed(j, vs.n)
    pool.consumed(j2, vs.n)
    P.sinc("vector",
           lambda e_: e_.tensor_mul(c1row[0:1, 0:NCH], gaterow[0:1, 0:NCH],
                                    alpharow[0:1, 0:NCH]), vs, 1)
    P.sinc("vector",
           lambda e_: e_.tensor_sub(c1row[0:1, 0:NCH], c1row[0:1, 0:NCH],
                                    gaterow[0:1, 0:NCH]), vs, 1)
    P.sinc("vector",
           lambda e_: e_.tensor_scalar(c1row[0:1, 0:NCH], c1row[0:1, 0:NCH],
                                       1.0, None, ALU.add), vs, 1)
    P.sinc("vector",
           lambda e_: e_.tensor_mul(c2row[0:1, 0:NCH], gaterow[0:1, 0:NCH],
                                    etarow[0:1, 0:NCH]), vs, 1)
    P.sinc("vector", lambda e_: e_.memset(scal[0:1, 0:1], 1.0), vs, 1)
    jm, bankm, fam = pool.take2()
    P.waitt("tensor", fam)
    for k in range(8):
        P.wait("scalar", vs, vs.n)
        P.wait("scalar", ss, ss.n)
        P.wait("scalar", mm, mm.n)
        P.sinc("scalar",
               lambda e_, k=k: e_.activation(gsc[:, 0:1024], M0T.ch(k),
                                             AF.Square), ss, 1)
        P.wait("tensor", ss, ss.n)
        P.inc("tensor",
              lambda e_, b=bankm, s=(k == 0):
              e_.matmul(b[0:1, 0:512], C.ones_b[:, :], gsc[:, 0:512],
                        start=s, stop=False), mm, 1)
        P.inc("tensor",
              lambda e_, b=bankm, z=(k == 7):
              e_.matmul(b[0:1, 0:512], C.ones_b[:, :], gsc[:, 512:1024],
                        start=False, stop=z), mm, 1)
    P.wait("vector", mm, mm.n)
    P.sinc("vector",
           lambda e_, b=bankm:
           e_.tensor_reduce(scal[0:1, 1:2], b[0:1, 0:512], AX.X,
                            ALU.add), vs, 1)
    pool.consumed(jm, vs.n)

    eps30 = C.cbias(1e-30)
    P.waitv("scalar", C.ready)
    for cj in range(NCH):
        c0 = CH * cj
        j, bank, fa = pool.take2()
        P.waitt("tensor", fa)
        P.wait("tensor", vs, vs.n)
        P.inc("tensor",
              lambda e_, b=bank: e_.matmul(b[0:128, 0:1], C.ones_row[:, :],
                                           scal[0:1, 0:1], start=True,
                                           stop=True), mm, 1)
        P.wait("vector", mm, mm.n)
        P.sinc("vector", lambda e_, b=bank: e_.tensor_copy(gb[:, 0:1],
                                                           b[0:128, 0:1]),
               vs, 1)
        pool.consumed(j, vs.n)
        if cj > 0:
            P.wait("vector", esb, esb.n)
            P.sinc("vector",
                   lambda e_:
                   e_.tensor_scalar(bcol[:, 0:1], b0col_t[:, 0:1],
                                    gb[0:32, 0:1], None, ALU.mult), vs, 1)
            P.sinc("vector",
                   lambda e_: e_.tensor_copy(bcolb[:, 0:1], bcol[:, 0:1]),
                   vs, 1)
        if cj > 0:
            j, bank, fa = pool.take2()
            P.waitt("tensor", fa)
            mm_accum(P, mm, bank,
                     [(KAb.chp(k, 0, 128, 0, cj), xn.ch(k, c0, c0 + CH), cj)
                      for k in range(8)])
            P.wait("vector", mm, mm.n)
            P.sinc("vector",
                   lambda e_, b=bank, cj=cj:
                   e_.tensor_scalar(kcb[0:cj, 0:CH], b[0:cj, 0:CH],
                                    bcol[0:cj, 0:1], None, ALU.mult), vs, 1)
            pool.consumed(j, vs.n)
        for oc in range(8):
            j, bank, fa = pool.take2()
            P.waitt("tensor", fa)
            if cj > 0:
                P.wait("tensor", vs, vs.n)
                mm_accum(P, mm, bank,
                         [(Etm[0:cj, 128 * oc:128 * (oc + 1)], kcb[0:cj, 0:CH],
                           128)])
                P.wait("vector", mm, mm.n)
                P.sinc("vector",
                       lambda e_, b=bank, oc=oc, c0=c0:
                       e_.scalar_tensor_tensor(outj.ch(oc), OB.ch(oc, c0, c0 + CH),
                                               gb[:, 0:1], b[0:128, 0:CH],
                                               ALU.mult, ALU.add), vs, 1)
            else:
                P.wait("vector", mm, mm.n)
                P.sinc("vector",
                       lambda e_, oc=oc, c0=c0:
                       e_.tensor_scalar(outj.ch(oc), OB.ch(oc, c0, c0 + CH),
                                        gb[:, 0:1], None, ALU.mult), vs, 1)
            pool.consumed(j, vs.n)
        for fc in range(8):
            j, bank, fa = pool.take2()
            P.waitt("tensor", fa)
            P.wait("tensor", vs, vs.n)
            mm_accum(P, mm, bank,
                     [(vg1T.chp(k, 0, 128, 128 * fc, 128 * (fc + 1)),
                       outj.ch(k), 128) for k in range(8)])
            P.wait("scalar", mm, mm.n)
            P.wait("scalar", vs, vs.n)
            P.sinc("scalar",
                   lambda e_, b=bank, fc=fc:
                   e_.activation(h1j.ch(fc), b[0:128, 0:CH], AF.Silu), ss, 1)
            pool.consumed(j, ss.n, ss)
        for oc in range(8):
            j, bank, fa = pool.take2()
            P.waitt("tensor", fa)
            P.wait("tensor", ss, ss.n)
            mm_accum(P, mm, bank,
                     [(vg2T.chp(k, 0, 128, 128 * oc, 128 * (oc + 1)),
                       h1j.ch(k), 128) for k in range(8)])
            P.wait("vector", mm, mm.n)
            P.sinc("vector",
                   lambda e_, b=bank, oc=oc:
                   e_.scalar_tensor_tensor(gsc[:, 0:2 * CH].bitcast(F32),
                                           b[0:128, 0:CH], 0.0, outj.ch(oc),
                                           ALU.add, ALU.add,
                                           accum_out=vt[:, oc:oc + 1]), vs, 1)
            P.sinc("vector",
                   lambda e_, oc=oc, c0=c0:
                   e_.tensor_add(x.ch(oc, c0, c0 + CH), x.ch(oc, c0, c0 + CH),
                                 gsc[:, 0:2 * CH].bitcast(F32)), vs, 1)
            pool.consumed(j, vs.n)
        if cj > 0:
            P.sinc("vector",
                   lambda e_, cj=cj:
                   e_.tensor_scalar(bcolb[0:cj, 1:2], Gsb[0:cj, cj:cj + 1],
                                    bcol[0:cj, 0:1], None, ALU.mult), vs, 1)
        for oc in range(8):
            if cj > 0:
                j, bank, fa = pool.take2()
                P.waitt("tensor", fa)
                P.wait("tensor", vs, vs.n)
                mm_accum(P, mm, bank,
                         [(Etm[0:cj, 128 * oc:128 * (oc + 1)],
                           bcolb[0:cj, 1:2], 128)])
                P.wait("vector", mm, mm.n)
                P.sinc("vector",
                       lambda e_, b=bank, oc=oc, cj=cj:
                       e_.scalar_tensor_tensor(vt[:, 8 + oc:9 + oc],
                                               M0K.ch(oc)[:, cj:cj + 1],
                                               gb[:, 0:1], b[0:128, 0:1],
                                               ALU.mult, ALU.add), vs, 1)
                pool.consumed(j, vs.n)
            else:
                P.sinc("vector",
                       lambda e_, oc=oc, cj=cj:
                       e_.tensor_scalar(vt[:, 8 + oc:9 + oc],
                                        M0K.ch(oc)[:, cj:cj + 1], gb[:, 0:1],
                                        None, ALU.mult), vs, 1)
        P.sinc("vector",
               lambda e_: e_.tensor_scalar(vt[:, 16:24], vt[:, 0:8],
                                           1.0 / CH, None, ALU.mult), vs, 1)
        P.sinc("vector",
               lambda e_: e_.tensor_sub(vt[:, 16:24], vt[:, 16:24],
                                        vt[:, 8:16]), vs, 1)
        P.sinc("vector",
               lambda e_: e_.tensor_copy(errb[:, 0:8], vt[:, 16:24]), vs, 1)
        P.sinc("vector",
               lambda e_: e_.tensor_copy(errb[:, 8:16], vt[:, 8:16]), vs, 1)
        for (rix, col) in ((2, 8), (3, 0)):
            j, bank, fa = pool.take2()
            P.waitt("tensor", fa)
            P.wait("tensor", vs, vs.n)
            mm_accum(P, mm, bank,
                     [(errb[:, k:k + 1], errb[:, col + k:col + k + 1], 1)
                      for k in range(8)])
            P.wait("vector", mm, mm.n)
            P.sinc("vector",
                   lambda e_, b=bank, rix=rix:
                   e_.tensor_copy(scal[0:1, rix:rix + 1], b[0:1, 0:1]), vs, 1)
            pool.consumed(j, vs.n)
        c1j = c1row[0:1, cj:cj + 1]
        c2j = c2row[0:1, cj:cj + 1]
        kkj = kkrow[0:1, cj:cj + 1]
        P.sinc("vector",
               lambda e_, c1j=c1j: e_.tensor_mul(scal[0:1, 6:7], c1j, c1j),
               vs, 1)
        P.sinc("vector",
               lambda e_: e_.tensor_mul(scal[0:1, 6:7], scal[0:1, 6:7],
                                        scal[0:1, 1:2]), vs, 1)
        P.sinc("vector",
               lambda e_, c1j=c1j: e_.tensor_mul(scal[0:1, 7:8], c1j,
                                                 scal[0:1, 2:3]), vs, 1)
        P.sinc("vector",
               lambda e_, c2j=c2j: e_.tensor_mul(scal[0:1, 7:8],
                                                 scal[0:1, 7:8], c2j), vs, 1)
        P.sinc("vector",
               lambda e_: e_.tensor_scalar(scal[0:1, 7:8], scal[0:1, 7:8],
                                           2.0, None, ALU.mult), vs, 1)
        P.sinc("vector",
               lambda e_, c2j=c2j: e_.tensor_mul(scal[0:1, 8:9], c2j, c2j),
               vs, 1)
        P.sinc("vector",
               lambda e_: e_.tensor_mul(scal[0:1, 8:9], scal[0:1, 8:9],
                                        scal[0:1, 3:4]), vs, 1)
        P.sinc("vector",
               lambda e_, kkj=kkj: e_.tensor_mul(scal[0:1, 8:9],
                                                 scal[0:1, 8:9], kkj), vs, 1)
        P.sinc("vector",
               lambda e_: e_.tensor_add(scal[0:1, 4:5], scal[0:1, 6:7],
                                        scal[0:1, 7:8]), vs, 1)
        P.sinc("vector",
               lambda e_: e_.tensor_add(scal[0:1, 4:5], scal[0:1, 4:5],
                                        scal[0:1, 8:9]), vs, 1)
        P.wait("scalar", vs, vs.n)
        P.sinc("scalar",
               lambda e_: e_.activation(scal[0:1, 5:6], scal[0:1, 4:5],
                                        AF.Ln, bias=eps30[0:1, :]),
               ss, 1)
        P.sinc("scalar",
               lambda e_: e_.activation(scal[0:1, 5:6], scal[0:1, 5:6],
                                        AF.Exp, scale=0.5), ss, 1)
        P.wait("vector", ss, ss.n)
        P.sinc("vector",
               lambda e_: e_.tensor_scalar(scal[0:1, 5:6], scal[0:1, 5:6],
                                           1e-6, None, ALU.add), vs, 1)
        P.sinc("vector",
               lambda e_: e_.reciprocal(scal[0:1, 5:6], scal[0:1, 5:6]),
               vs, 1)
        P.sinc("vector",
               lambda e_: e_.tensor_scalar(scal[0:1, 5:6], scal[0:1, 5:6],
                                           30.0, None, ALU.mult), vs, 1)
        P.sinc("vector",
               lambda e_: e_.tensor_scalar(scal[0:1, 5:6], scal[0:1, 5:6],
                                           1.0, None, ALU.min), vs, 1)
        P.sinc("vector",
               lambda e_, c1j=c1j: e_.tensor_mul(scal[0:1, 0:1],
                                                 scal[0:1, 0:1], c1j), vs, 1)
        P.sinc("vector",
               lambda e_: e_.tensor_mul(scal[0:1, 0:1], scal[0:1, 0:1],
                                        scal[0:1, 5:6]), vs, 1)
        P.sinc("vector",
               lambda e_: e_.tensor_mul(scal[0:1, 6:7], scal[0:1, 5:6],
                                        scal[0:1, 5:6]), vs, 1)
        P.sinc("vector",
               lambda e_: e_.tensor_mul(scal[0:1, 1:2], scal[0:1, 6:7],
                                        scal[0:1, 4:5]), vs, 1)
        P.sinc("vector",
               lambda e_, c2j=c2j: e_.tensor_mul(scal[0:1, 9:10],
                                                 scal[0:1, 5:6], c2j), vs, 1)
        P.sinc("vector",
               lambda e_: e_.reciprocal(scal[0:1, 10:11], scal[0:1, 0:1]),
               vs, 1)
        P.sinc("vector",
               lambda e_: e_.tensor_mul(scal[0:1, 11:12], scal[0:1, 9:10],
                                        scal[0:1, 10:11]), vs, 1)
        P.wait("sync", vs, vs.n)
        P.dma("sync", b0col_t[cj:cj + 1, 0:1], scal[0:1, 11:12], esb)
        for oc in range(8):
            j, bank, fa = pool.take2()
            P.waitt("tensor", fa)
            P.wait("tensor", vs, vs.n)
            P.wait("tensor", esb, esb.n)
            P.inc("tensor",
                  lambda e_, b=bank, oc=oc:
                  e_.transpose(b[0:1, 0:128].bitcast(BF16)[0:1, 0:128],
                               errb[:, oc:oc + 1], C.eye[:, :]), mm, 1)
            P.wait("vector", mm, mm.n)
            P.sinc("vector",
                   lambda e_, b=bank, oc=oc:
                   e_.tensor_copy(rows[0:1, 128 * oc:128 * (oc + 1)],
                                  b[0:1, 0:128].bitcast(BF16)[0:1, 0:128]),
                   vs, 1)
            pool.consumed(j, vs.n)
        P.wait("sync", vs, vs.n)
        P.dma("sync", Etm[cj:cj + 1, :], rows[0:1, 0:1024], esb)
    return (vs, vs.n)


def build_l2attn_launch(T=2048):
    names = [n for n in SHAPES if n.startswith("L2_attn")]
    def body(env):
        B, P, C = env["B"], env["P"], env["C"]
        pt0 = B.ps("pt0", [128, 128], BF16)
        pt1 = B.ps("pt1", [128, 128], BF16)
        wt = WT(P, B, C, pt0, pt1)
        return emit_l2attn(P, B, C, env["L"], env["pool"], wt,
                           env["x"], env["xn"], env["T"], env["dram"],
                           env["x_ready"])
    return make_launch(names, body, T, npool=3)


def build_l3attn_launch(T=2048, SB=8):
    from contextlib import ExitStack
    PP = T // SB
    NT = T // 128
    nc = bass.Bass("TRN2", target_bir_lowering=False)
    XNT = nc.dram_tensor("XNT", [T, 1024], BF16, kind="ExternalInput").ap()
    A3T = nc.dram_tensor("A3T", [T, 1024], BF16, kind="ExternalOutput").ap()
    dram = {nm: nc.dram_tensor(nm, SHAPES[nm], F32, kind="ExternalInput").ap()
            for nm in ["L3_attn_in_proj_weight", "L3_attn_out_weight"]}
    ctx = ExitStack()
    with ctx:
        B = Bufs(nc, ctx)
        P = Prog(nc, ctx)
        C = Consts(P, B)
        pt0 = B.ps("pt0", [128, 128], BF16)
        pt1 = B.ps("pt1", [128, 128], BF16)
        ptx = [B.ps(f"ptx{i}", [128, 128], BF16) for i in range(2)]
        pk = [B.ps(f"pk{i}", [128, 512], F32) for i in range(3)]
        ev = P.sem("ev")
        pool = Pool(pk, ev)
        wt = WT(P, B, C, pt0, pt1)
        mm = P.sem("mm")
        ts = P.sem("ts")
        vs = P.sem("vs")
        gs = P.sem("gs")
        ss = P.sem("ss")
        ld = P.sem("ld")
        st = P.sem("st")

        WinT, _ = wt.load_T("l3a_win", dram["L3_attn_in_proj_weight"],
                            3072, 1024)

        xn3 = B.fm("l3a_xn", T, 8, BF16)
        q3 = B.fm("l3a_q", T, 8, BF16)
        k3 = B.fm("l3a_k", T, 8, BF16)
        v3 = B.fm("l3a_v", T, 8, BF16)
        att = FM(xn3.t, T, 8)
        a3 = FM(q3.t, T, 8)
        WoT = FM(k3.t[:, 0:8 * 1024], 1024, 8, nfeat=1024)
        xstg = B.sb("l3a_xstg", [128, 2048], BF16)
        ostg = xstg
        Ebuf = B.sb("l3a_E", [128, 2 * 8 * PP], BF16)
        wsp = WinT.t
        SC = wsp[0:16, 0:2 * T].bitcast(F32)
        SCE = wsp[0:16, 2 * T:4 * T].bitcast(F32)
        M3 = wsp[0:16, 4 * T:4 * T + 8 * PP].bitcast(F32)
        R3 = wsp[0:16, 4 * T + 8 * PP:4 * T + 10 * PP].bitcast(F32)
        PRB = wsp[0:16, 4 * T + 10 * PP:5 * T + 10 * PP]
        Pb = wsp[0:128, 5 * T + 10 * PP:5 * T + 12 * PP]
        tmp8 = wsp[0:128, 5 * T + 12 * PP:5 * T + 14 * PP]
        acc = FM(wsp[0:128, 5 * T + 14 * PP:5 * T + 14 * PP + 16 * PP]
                 .bitcast(F32), PP, 8)

        BD = B.sb("l3a_bd", [128, 8 * 16], BF16)
        BU = B.sb("l3a_bu", [16, 8 * 128], BF16)
        for c in range(8):
            P.sinc("gpsimd",
                   lambda e, c=c: e.memset(BD[:, 16 * c:16 * (c + 1)], 1.0),
                   gs, 1)
            P.sinc("gpsimd",
                   lambda e, c=c: e.affine_select(
                       BD[:, 16 * c:16 * (c + 1)], BD[:, 16 * c:16 * (c + 1)],
                       pattern=[[-64, 16]], compare_op=ALU.is_ge, fill=0.0,
                       base=128 * c, channel_multiplier=1), gs, 1)
            P.sinc("gpsimd",
                   lambda e, c=c: e.affine_select(
                       BD[:, 16 * c:16 * (c + 1)], BD[:, 16 * c:16 * (c + 1)],
                       pattern=[[64, 16]], compare_op=ALU.is_ge, fill=0.0,
                       base=-128 * c + 63, channel_multiplier=-1), gs, 1)
            P.sinc("gpsimd",
                   lambda e, c=c: e.memset(BU[:, 128 * c:128 * (c + 1)], 1.0),
                   gs, 1)
            P.sinc("gpsimd",
                   lambda e, c=c: e.affine_select(
                       BU[:, 128 * c:128 * (c + 1)],
                       BU[:, 128 * c:128 * (c + 1)],
                       pattern=[[1, 128]], compare_op=ALU.is_ge, fill=0.0,
                       base=128 * c, channel_multiplier=-64), gs, 1)
            P.sinc("gpsimd",
                   lambda e, c=c: e.affine_select(
                       BU[:, 128 * c:128 * (c + 1)],
                       BU[:, 128 * c:128 * (c + 1)],
                       pattern=[[-1, 128]], compare_op=ALU.is_ge, fill=0.0,
                       base=-128 * c + 63, channel_multiplier=64), gs, 1)
        bd_rdy = (gs, gs.n)

        ld2 = [P.sem("ld2a"), P.sem("ld2b")]
        for tt in range(NT):
            sl = tt % 2
            stg = xstg[:, 1024 * sl:1024 * (sl + 1)]
            if tt >= 2:
                P.wait("sync", ss, 8 * (tt - 1))
            P.dma("sync", stg, XNT[128 * tt:128 * (tt + 1), :], ld2[sl])
            ldv = ld2[sl].n
            for k in range(8):
                j, bank, fa = (tt * 8 + k) % 2, ptx[(tt * 8 + k) % 2], None
                P.wait("tensor", ld2[sl], ldv)
                P.waitv("tensor", C.ready)
                if tt * 8 + k >= 2:
                    P.wait("tensor", ss, tt * 8 + k - 1)
                P.inc("tensor",
                      lambda e, b=bank, stg=stg, k=k:
                      e.transpose(b[0:128, 0:128],
                                  stg[:, 128 * k:128 * (k + 1)],
                                  C.eye[:, :]), ts, 1)
                P.wait("scalar", ts, ts.n)
                P.inc("scalar",
                      lambda e, b=bank, k=k, tt=tt:
                      e.copy(xn3.chp(k, 0, 128, 128 * tt, 128 * (tt + 1)),
                             b[0:128, 0:128]), ss, 1)
        xn_rdy = (ss, ss.n)

        NT5 = T // 512
        qkv_done = None
        for fc in range(24):
            for t5 in range(NT5):
                j, bank, fa = pool.take2()
                P.waitt("tensor", fa)
                if fc == 0 and t5 == 0:
                    P.waitv("tensor", xn_rdy)
                mm_accum(P, mm, bank,
                         [(WinT.chp(k, 0, 128, 128 * fc, 128 * (fc + 1)),
                           xn3.ch(k, 512 * t5, 512 * (t5 + 1)), 128)
                          for k in range(8)])
                P.wait("scalar", mm, mm.n)
                for r0 in (0, 64):
                    f = 128 * fc + r0
                    h, j3, dh = f // 192, (f % 192) // 64, f % 64
                    fmj = (q3, k3, v3)[j3]
                    kk, pb = h // 2, 64 * (h % 2)
                    P.inc("scalar",
                          lambda e, b=bank, r0=r0, fmj=fmj, kk=kk, pb=pb,
                                 t5=t5:
                          e.copy(fmj.chp(kk, pb, pb + 64, 512 * t5,
                                         512 * (t5 + 1)),
                                 b[r0:r0 + 64, 0:512]), ss, 1)
                pool.consumed(j, ss.n, ss)
        qkv_rdy = (ss, ss.n)

        ET = 8 * PP
        scores_done = {}
        for s in range(SB):
            for t in range(SB):
                i = s * SB + t
                sl = i % 2
                E = FM(Ebuf[:, ET * sl:ET * (sl + 1)], PP, 8)
                for c in range(8):
                    eng = "vector" if c < 4 else "gpsimd"
                    sem = vs if c < 4 else gs
                    if s == 0 and t == 0 and c in (0, 4):
                        P.waitv(eng, qkv_rdy)
                    if i >= 2 and c in (0, 4):
                        P.wait(eng, mm, scores_done[i - 2])
                    P.sinc(eng,
                           lambda e, c=c, E=E, s=s, t=t:
                           e.tensor_mul(E.ch(c), q3.ch(c, PP * s, PP * (s + 1)),
                                        k3.ch(c, PP * t, PP * (t + 1))),
                           sem, 1)
                j, bank, fa = pool.take2()
                P.waitt("tensor", fa)
                P.wait("tensor", vs, vs.n)
                P.wait("tensor", gs, gs.n)
                if s == 0 and t == 0:
                    P.waitv("tensor", bd_rdy)
                mm_accum(P, mm, bank,
                         [(BD[:, 16 * c:16 * (c + 1)], E.ch(c), 16)
                          for c in range(8)])
                scores_done[i] = mm.n
                P.wait("scalar", mm, mm.n)
                P.inc("scalar",
                      lambda e, b=bank, t=t:
                      e.copy(SC[:, PP * t:PP * (t + 1)], b[0:16, 0:PP]),
                      ss, 1)
                pool.consumed(j, ss.n, ss)
            P.wait("vector", ss, ss.n)
            half = SB // 2 * PP
            P.sinc("vector",
                   lambda e: e.tensor_max(SCE[:, 0:half], SC[:, 0:half],
                                          SC[:, half:2 * half]), vs, 1)
            P.sinc("vector",
                   lambda e: e.tensor_max(SCE[:, 0:half // 2], SCE[:, 0:half // 2],
                                          SCE[:, half // 2:half]), vs, 1)
            P.sinc("vector",
                   lambda e: e.tensor_max(M3[:, 0:PP], SCE[:, 0:PP],
                                          SCE[:, PP:2 * PP]), vs, 1)
            P.sinc("vector",
                   lambda e: e.tensor_sub(
                       SCE[:, 0:T].rearrange("p (t q) -> p t q", q=PP),
                       SC[:, 0:T].rearrange("p (t q) -> p t q", q=PP),
                       M3[:, 0:PP].rearrange("p (one q) -> p one q", one=1)
                       .broadcast_to((16, SB, PP))), vs, 1)
            P.wait("scalar", vs, vs.n)
            P.sinc("scalar",
                   lambda e: e.activation(SCE[:, 0:T], SCE[:, 0:T], AF.Exp,
                                          scale=0.125), ss, 1)
            P.wait("vector", ss, ss.n)
            P.sinc("vector",
                   lambda e: e.tensor_add(M3[:, PP:3 * PP], SCE[:, 0:2 * PP],
                                          SCE[:, 2 * PP:4 * PP]), vs, 1)
            P.sinc("vector",
                   lambda e: e.tensor_add(M3[:, PP:3 * PP], M3[:, PP:3 * PP],
                                          SCE[:, 4 * PP:6 * PP]), vs, 1)
            P.sinc("vector",
                   lambda e: e.tensor_add(M3[:, PP:3 * PP], M3[:, PP:3 * PP],
                                          SCE[:, 6 * PP:8 * PP]), vs, 1)
            P.sinc("vector",
                   lambda e: e.tensor_add(M3[:, 3 * PP:4 * PP],
                                          M3[:, PP:2 * PP],
                                          M3[:, 2 * PP:3 * PP]), vs, 1)
            P.sinc("vector",
                   lambda e: e.reciprocal(R3[:, 0:PP], M3[:, 3 * PP:4 * PP]),
                   vs, 1)
            P.sinc("vector",
                   lambda e: e.tensor_mul(
                       PRB[:, 0:T].rearrange("p (t q) -> p t q", q=PP),
                       SCE[:, 0:T].rearrange("p (t q) -> p t q", q=PP),
                       R3[:, 0:PP].rearrange("p (one q) -> p one q", one=1)
                       .broadcast_to((16, SB, PP))), vs, 1)
            prb_rdy = vs.n
            pb_used = {}
            for t in range(SB):
                for c in range(8):
                    i8 = t * 8 + c
                    j, bank, fa = pool.take2()
                    P.waitt("tensor", fa)
                    P.wait("tensor", vs, prb_rdy)
                    sl = i8 % 2
                    mm_accum(P, mm, bank,
                             [(BU[:, 128 * c:128 * (c + 1)],
                               PRB[:, PP * t:PP * (t + 1)], 128)])
                    P.wait("scalar", mm, mm.n)
                    if i8 >= 2:
                        P.wait("scalar", vs, pb_used[i8 - 2])
                    P.inc("scalar",
                          lambda e, b=bank, sl=sl:
                          e.copy(Pb[:, PP * sl:PP * (sl + 1)], b[0:128, 0:PP]),
                          ss, 1)
                    pool.consumed(j, ss.n, ss)
                    P.wait("vector", ss, ss.n)
                    if t == 0:
                        P.sinc("vector",
                               lambda e, c=c, sl=sl, t=t:
                               e.tensor_mul(acc.ch(c),
                                            v3.ch(c, PP * t, PP * (t + 1)),
                                            Pb[:, PP * sl:PP * (sl + 1)]),
                               vs, 1)
                    else:
                        P.sinc("vector",
                               lambda e, c=c, sl=sl, t=t:
                               e.tensor_mul(tmp8[:, PP * sl:PP * (sl + 1)],
                                            v3.ch(c, PP * t, PP * (t + 1)),
                                            Pb[:, PP * sl:PP * (sl + 1)]),
                               vs, 1)
                        P.sinc("vector",
                               lambda e, c=c, sl=sl:
                               e.tensor_add(acc.ch(c), acc.ch(c),
                                            tmp8[:, PP * sl:PP * (sl + 1)]),
                               vs, 1)
                    pb_used[i8] = vs.n
            for c in range(8):
                P.wait("vector", vs, vs.n)
                P.sinc("vector",
                       lambda e, c=c, s=s:
                       e.tensor_copy(att.ch(c, PP * s, PP * (s + 1)),
                                     acc.ch(c)), vs, 1)
        att_rdy = (vs, vs.n)

        P.wait("scalar", vs, vs.n)
        P.wait("scalar", gs, gs.n)
        _, wo_rdy = wt.load_T("l3a_wo", dram["L3_attn_out_weight"],
                              1024, 1024, out_fm=WoT)

        for fc in range(8):
            for t5 in range(NT5):
                j, bank, fa = pool.take2()
                P.waitt("tensor", fa)
                if fc == 0 and t5 == 0:
                    P.waitv("tensor", att_rdy)
                    P.waitv("tensor", wo_rdy)
                mm_accum(P, mm, bank,
                         [(WoT.chp(k, 0, 128, 128 * fc, 128 * (fc + 1)),
                           att.ch(k, 512 * t5, 512 * (t5 + 1)), 128)
                          for k in range(8)])
                P.wait("scalar", mm, mm.n)
                P.inc("scalar",
                      lambda e, b=bank, fc=fc, t5=t5:
                      e.copy(a3.ch(fc, 512 * t5, 512 * (t5 + 1)),
                             b[0:128, 0:512]), ss, 1)
                pool.consumed(j, ss.n, ss)
        a3_rdy = (ss, ss.n)

        st2 = [P.sem("st2a"), P.sem("st2b")]
        for tt in range(NT):
            sl = tt % 2
            stg = ostg[:, 1024 * sl:1024 * (sl + 1)]
            if tt >= 2:
                P.wait("tensor", st2[sl], st2[sl].n)
            for k in range(8):
                bank = ptx[(tt * 8 + k) % 2]
                if tt == 0 and k == 0:
                    P.waitv("tensor", a3_rdy)
                P.wait("tensor", ss, ss.n)
                if tt * 8 + k >= 2:
                    P.wait("tensor", gs, gs.n)
                P.inc("tensor",
                      lambda e, b=bank, k=k, tt=tt:
                      e.transpose(b[0:128, 0:128],
                                  a3.chp(k, 0, 128, 128 * tt, 128 * (tt + 1)),
                                  C.eye[:, :]), ts, 1)
                P.wait("vector", ts, ts.n)
                P.sinc("vector",
                       lambda e, b=bank, stg=stg, k=k:
                       e.tensor_copy(stg[:, 128 * k:128 * (k + 1)],
                                     b[0:128, 0:128]), gs, 1)
            P.wait("sync", gs, gs.n)
            P.dma("sync", A3T[128 * tt:128 * (tt + 1), :], stg, st2[sl])
        P.wait("sync", st2[0], st2[0].n)
        P.wait("sync", st2[1], st2[1].n)
        P.emit()
    return nc


def build_ln3a_launch(T=2048):
    from contextlib import ExitStack
    nc = bass.Bass("TRN2", target_bir_lowering=False)
    XIN = nc.dram_tensor("XIN", [1024, T], F32, kind="ExternalInput").ap()
    XNT = nc.dram_tensor("XNT", [T, 1024], BF16, kind="ExternalOutput").ap()
    NT = T // 128
    ctx = ExitStack()
    with ctx:
        B = Bufs(nc, ctx)
        P = Prog(nc, ctx)
        C = Consts(P, B)
        p_row = B.ps("p_row", [1, 512], F32)
        p_bm = B.ps("p_bm", [128, 512], F32)
        p_br = B.ps("p_br", [128, 512], F32)
        ptx = [B.ps(f"ptx{i}", [128, 128], BF16) for i in range(2)]
        L = LNCtx(P, B, C, p_row, p_bm, p_br)
        ld = P.sem("ld")
        st = P.sem("st")
        ts = P.sem("ts")
        ss = P.sem("ss")
        x, x_ready = load_x(P, B, C, XIN, T, ld)
        xn = B.fm("xn", T, 8, BF16)
        rdy = emit_ln(L, x, xn, T, x_ready)
        stg = B.sb("ln3_stg", [128, 2048], BF16)
        st2 = [P.sem("st2a"), P.sem("st2b")]
        for tt in range(NT):
            sl = tt % 2
            sg = stg[:, 1024 * sl:1024 * (sl + 1)]
            if tt >= 2:
                P.wait("tensor", st2[sl], st2[sl].n)
            for k in range(8):
                bank = ptx[(tt * 8 + k) % 2]
                if tt == 0 and k == 0:
                    P.waitv("tensor", rdy)
                if tt * 8 + k >= 2:
                    P.wait("tensor", ss, tt * 8 + k - 2 + 1)
                P.inc("tensor",
                      lambda e, b=bank, k=k, tt=tt:
                      e.transpose(b[0:128, 0:128],
                                  xn.chp(k, 0, 128, 128 * tt, 128 * (tt + 1)),
                                  C.eye[:, :]), ts, 1)
                P.wait("scalar", ts, ts.n)
                P.inc("scalar",
                      lambda e, b=bank, sg=sg, k=k:
                      e.copy(sg[:, 128 * k:128 * (k + 1)], b[0:128, 0:128]),
                      ss, 1)
            P.wait("sync", ss, ss.n)
            P.dma("sync", XNT[128 * tt:128 * (tt + 1), :], sg, st2[sl])
        P.wait("sync", st2[0], st2[0].n)
        P.wait("sync", st2[1], st2[1].n)
        P.emit()
    return nc


I16 = mybir.dt.int16
I32 = mybir.dt.int32
MAGIC = float(3 * 2 ** 22)


def build_l3ffn_launch(T=2048, NC=8, dbg=0, stats_only=False,
                       ext_stats=False):
    from contextlib import ExitStack
    NT = T // 128
    NTOT = NC * T * 8
    TH = T // 2
    nc = bass.Bass("TRN2", target_bir_lowering=False)
    XIN = nc.dram_tensor("XIN", [1024, T], F32, kind="ExternalInput").ap()
    A3T = nc.dram_tensor("A3T", [T, 1024], BF16, kind="ExternalInput").ap()
    XOUT = nc.dram_tensor("XOUT", [1024, T], F32, kind="ExternalOutput").ap()
    dram = {nm: nc.dram_tensor(nm, SHAPES[nm], F32, kind="ExternalInput").ap()
            for nm in SHAPES if nm.startswith("L3_ffn")}
    CAT = nc.dram_tensor("CATB", [16384, 2048], BF16, kind="Internal").ap()
    ARI = nc.dram_tensor("ARI", [128, 4], F32, kind="Internal").ap()
    DBG = nc.dram_tensor("DBG", [128, 2048], F32, kind="ExternalOutput").ap() \
        if dbg else None
    ARO = nc.dram_tensor("ARO", [128, 4], F32, kind="Internal").ap()
    RG = [list(range(NC))]
    STATSOUT = (nc.dram_tensor("STATSOUT", [128, 4], F32,
                               kind="ExternalOutput").ap()
                if stats_only else None)
    STATS8 = (nc.dram_tensor("STATS8", [NC * 128, 4], F32,
                             kind="ExternalInput").ap() if ext_stats else None)
    ctx = ExitStack()
    with ctx:
        B = Bufs(nc, ctx)
        P = Prog(nc, ctx)
        C = Consts(P, B)
        p_row = B.ps("p_row", [1, 512], F32)
        p_bm = B.ps("p_bm", [128, 512], F32)
        p_br = B.ps("p_br", [128, 512], F32)
        pk = [B.ps(f"pk{i}", [128, 512], F32) for i in range(3)]
        pc0 = B.ps("pc0", [128, 512], F32)
        pc1 = B.ps("pc1", [128, 512], F32)
        ev = P.sem("ev")
        pool = Pool(pk, ev)
        L = LNCtx(P, B, C, p_row, p_bm, p_br)
        mm = P.sem("mm")
        ts = P.sem("ts")
        vs = P.sem("vs")
        gs = P.sem("gs")
        ss = P.sem("ss")
        ld = P.sem("ld")
        st = P.sem("st")
        gld = P.sem("gld")
        cst = P.sem("cst")
        cwr = P.sem("cwr")
        gat4 = [P.sem(f"gat{i}") for i in range(4)]
        shf = P.sem("shf")
        arS = P.sem("arS")

        BANDM = B.sb("l3f_band", [128, 64], F32)
        REPM = B.sb("l3f_rep", [64, 128], BF16)
        P.waitv("scalar", C.ready)
        for b4 in range(2):
            P.sinc("scalar",
                   lambda e, b4=b4:
                   e.copy(BANDM[64 * b4:64 * (b4 + 1), :],
                          C.eye[0:64, 0:64]), gs, 1)
            P.sinc("scalar",
                   lambda e, b4=b4:
                   e.copy(REPM[:, 64 * b4:64 * (b4 + 1)],
                          C.eye[0:64, 0:64]), gs, 1)
        ncol = B.sb("l3f_ncol", [128, 2], F32)
        nci = ncol[:, 1:2].bitcast(I32)[:, 0:1]
        P.sinc("gpsimd",
               lambda e: e.iota(nci, pattern=[[0, 1]], base=0,
                                channel_multiplier=1), gs, 1)
        P.wait("vector", gs, gs.n)
        P.sinc("vector",
               lambda e: e.tensor_copy(ncol[:, 0:1], nci), gs, 1)
        c5b = C.cbias(1e-5)
        c30b = C.cbias(1e-30)
        const_rdy = (gs, gs.n)

        bqt = load_bias_cols(P, B, "l3f_bq", dram["L3_ffn_bq"], 2048, gld)
        gam = load_vec(P, B, "l3f_gam", dram["L3_ffn_bn_gamma"], 256, gld,
                       eng="scalar")
        bet = load_vec(P, B, "l3f_bet", dram["L3_ffn_bn_beta"], 256, gld,
                       eng="scalar")
        sk0 = B.sb("l3f_sk0", [128, 128], F32)
        sk1 = B.sb("l3f_sk1", [128, 128], F32)
        P.dma("scalar", sk0[:], dram["L3_ffn_sub_keys0"], gld)
        P.dma("scalar", sk1[:], dram["L3_ffn_sub_keys1"], gld)
        wv_rdy = (gld, gld.n)

        x, x_ready = load_x(P, B, C, XIN, T, ld)
        xn = B.fm("xn", T, 8, BF16)
        q = B.fm("l3f_q", TH, 16, BF16)
        wq = B.sb("l3f_wq", [128, 8 * 2048], BF16)
        e_all = B.sb("l3f_eall", [128, NT * 128], F32)
        w_all = B.sb("l3f_wall", [128, NT * 128], F32)
        selb = B.sb("l3f_selb", [128, 1024], F32)
        sknb = B.sb("l3f_sknb", [128, 2 * 128], BF16)
        sknT = B.sb("l3f_sknT", [128, 2 * 128], BF16)
        stats = B.sb("l3f_st", [128, 112], F32)
        catstg = B.sb("l3f_cstg", [128, 2048], BF16)
        qsp = q.t
        dst4 = [qsp[:, 2048 * i:2048 * (i + 1)] for i in range(4)]
        XR2 = [qsp[:, 8192 + 1024 * i:8192 + 1024 * (i + 1)]
               for i in range(2)]
        xnslab2 = [qsp[0:64, 10240 + 1024 * i:10240 + 1024 * (i + 1)]
                   for i in range(2)]
        dscr = qsp[:, 12288:13312]
        outtm = qsp[:, 13312:15360].bitcast(F32)
        ECOLf = qsp[:, 15360:15616].bitcast(F32)
        ICOL32 = qsp[:, 15616:15872].bitcast(I32)
        WCOL = qsp[:, 15872:16128].bitcast(F32)
        zcol2 = [qsp[:, 16128:16192], qsp[:, 16196:16260]]
        eic2 = qsp[:, 16260:16264].bitcast(F32)
        gec2 = qsp[:, 16264:16268].bitcast(F32)
        a3stg = wq[:, 0:2048]
        ksc = L.apl
        s0k = wq[:, 0:8192].bitcast(F32)
        s1k = wq[:, 8192:16384].bitcast(F32)
        selbv = selb[:, :]
        m8s = selbv[:, 768:776]
        exw = selbv[:, 776:792]
        lcc = selbv[:, 792:794]

        ncat = 16384 // 128
        for r in range(ncat if not stats_only else 0):
            cs = catstg
            if r >= 1:
                P.wait("gpsimd", cwr, 16 * r)
            P.dma("gpsimd", cs[:, 0:1024],
                  dram["L3_ffn_expert_down"][128 * r:128 * (r + 1), :], cst)
            P.dma("gpsimd", cs[:, 1024:2048],
                  dram["L3_ffn_expert_up"][128 * r:128 * (r + 1), :], cst)
            P.wait("sync", cst, cst.n)
            P.dma("sync", CAT[128 * r:128 * (r + 1), :], cs[:, :], cwr)
        cat_rdy = (cwr, cwr.n)

        lda = [P.sem("lda0"), P.sem("lda1")]
        for tt in range(NT):
            sl = tt % 2
            sg = a3stg[:, 1024 * sl:1024 * (sl + 1)]
            if tt >= 2:
                P.wait("sync", ts, 8 * (tt - 1))
            P.dma("sync", sg, A3T[128 * tt:128 * (tt + 1), :], lda[sl])
            ldv = lda[sl].n
            for k in range(8):
                i8 = tt * 8 + k
                bank = [pc0, pc1][i8 % 2]
                bb = bank.bitcast(BF16)
                P.wait("tensor", lda[sl], ldv)
                P.waitv("tensor", C.ready)
                if i8 >= 2:
                    P.wait("tensor", vs, i8 - 1)
                P.inc("tensor",
                      lambda e, bb=bb, sg=sg, k=k:
                      e.transpose(bb[0:128, 0:128],
                                  sg[:, 128 * k:128 * (k + 1)],
                                  C.eye[:, :]), ts, 1)
                P.wait("vector", ts, ts.n)
                if i8 == 0:
                    P.waitv("vector", x_ready)
                P.sinc("vector",
                       lambda e, bb=bb, k=k, tt=tt:
                       e.tensor_add(x.chp(k, 0, 128, 128 * tt, 128 * (tt + 1)),
                                    x.chp(k, 0, 128, 128 * tt, 128 * (tt + 1)),
                                    bb[0:128, 0:128]), vs, 1)
        xa_rdy = (vs, vs.n)

        xn_rdy = emit_ln(L, x, xn, T, xa_rdy)

        wqs = P.sem("wqs")

        def load_wq(war_vs):
            P.wait("gpsimd", ts, ts.n)
            P.wait("gpsimd", wqs, wqs.n)
            P.wait("gpsimd", mm, mm.n)
            if war_vs:
                P.wait("gpsimd", vs, war_vs)
            for dc in range(8):
                P.dma("gpsimd", wq[:, 2048 * dc:2048 * (dc + 1)],
                      dram["L3_ffn_Wq"][128 * dc:128 * (dc + 1), :], wqs)
            return (wqs, wqs.n)

        def qproj(half, wqr, with_bias):
            tb = half * TH
            for fc in range(16):
                for t5 in range(TH // 512):
                    j, bank, fa = pool.take2()
                    P.waitt("tensor", fa)
                    if fc == 0 and t5 == 0:
                        P.waitv("tensor", xn_rdy)
                        P.waitv("tensor", wqr)
                    mm_accum(P, mm, bank,
                             [(wq[:, 2048 * k + 128 * fc:
                                  2048 * k + 128 * (fc + 1)],
                               xn.ch(k, tb + 512 * t5, tb + 512 * (t5 + 1)),
                               128) for k in range(8)])
                    P.wait("vector", mm, mm.n)
                    if fc == 0 and t5 == 0:
                        P.waitv("vector", wv_rdy)
                    if with_bias:
                        P.sinc("vector",
                               lambda e, b=bank, fc=fc, t5=t5:
                               e.tensor_scalar(
                                   q.ch(fc, 512 * t5, 512 * (t5 + 1)),
                                   b[0:128, 0:512],
                                   bqt[:, fc:fc + 1], None, ALU.add), vs, 1)
                    else:
                        P.sinc("vector",
                               lambda e, b=bank, fc=fc, t5=t5:
                               e.tensor_copy(
                                   q.ch(fc, 512 * t5, 512 * (t5 + 1)),
                                   b[0:128, 0:512]), vs, 1)
                    pool.consumed(j, vs.n, vs)

        wqr = load_wq(None)
        for half in (0, 1):
            qproj(half, wqr, True)
            c0 = 16 * half
            for fc in range(16):
                P.sinc("vector",
                       lambda e, fc=fc, c0=c0:
                       e.scalar_tensor_tensor(q.ch(fc), q.ch(fc), 1.0,
                                              q.ch(fc), ALU.mult, ALU.bypass,
                                              accum_out=stats[:, c0 + fc:
                                                              c0 + fc + 1]),
                       vs, 1)
                P.wait("scalar", vs, vs.n)
                P.sinc("scalar",
                       lambda e, fc=fc:
                       e.activation(L.sq[:, 0:512], q.ch(fc, 0, 512),
                                    AF.Square,
                                    accum_out=stats[:, 110:111]), ss, 1)
                P.sinc("scalar",
                       lambda e, fc=fc:
                       e.activation(L.sq[:, 512:1024], q.ch(fc, 512, 1024),
                                    AF.Square,
                                    accum_out=stats[:, 111:112]), ss, 1)
                P.wait("vector", ss, ss.n)
                P.sinc("vector",
                       lambda e, fc=fc, c0=c0:
                       e.tensor_add(stats[:, 32 + c0 + fc:33 + c0 + fc],
                                    stats[:, 110:111], stats[:, 111:112]),
                       vs, 1)
        P.sinc("vector",
               lambda e: e.tensor_add(stats[:, 64:80], stats[:, 0:16],
                                      stats[:, 16:32]), vs, 1)
        P.sinc("vector",
               lambda e: e.tensor_add(stats[:, 80:96], stats[:, 32:48],
                                      stats[:, 48:64]), vs, 1)
        P.sinc("vector",
               lambda e: e.tensor_add(stats[:, 96:104], stats[:, 80:88],
                                      stats[:, 88:96]), vs, 1)
        P.sinc("vector",
               lambda e: e.tensor_add(stats[:, 96:100], stats[:, 96:100],
                                      stats[:, 100:104]), vs, 1)
        P.sinc("vector",
               lambda e: e.tensor_add(stats[:, 106:108], stats[:, 96:98],
                                      stats[:, 98:100]), vs, 1)
        P.sinc("vector",
               lambda e: e.tensor_add(stats[:, 96:104], stats[:, 64:72],
                                      stats[:, 72:80]), vs, 1)
        P.sinc("vector",
               lambda e: e.tensor_add(stats[:, 96:100], stats[:, 96:100],
                                      stats[:, 100:104]), vs, 1)
        P.sinc("vector",
               lambda e: e.tensor_add(stats[:, 104:106], stats[:, 96:98],
                                      stats[:, 98:100]), vs, 1)
        if stats_only:
            P.wait("sync", vs, vs.n)
            P.dma("sync", STATSOUT, stats[:, 104:108], st)
            P.wait("sync", st, st.n)
            P.emit()
            return nc
        if ext_stats:
            P.wait("sync", vs, vs.n)
            for r in range(NC):
                P.dma("sync", stats[:, 4 * r:4 * (r + 1)],
                      STATS8[128 * r:128 * (r + 1), :], arS)
            P.wait("vector", arS, arS.n)
            n8 = NC
            while n8 > 2:
                h8 = n8 // 2
                P.sinc("vector",
                       lambda e, h8=h8:
                       e.tensor_add(stats[:, 0:4 * h8], stats[:, 0:4 * h8],
                                    stats[:, 4 * h8:8 * h8]), vs, 1)
                n8 = h8
            P.sinc("vector",
                   lambda e: e.tensor_add(stats[:, 104:108], stats[:, 0:4],
                                          stats[:, 4:8]), vs, 1)
        else:
            P.wait("sync", vs, vs.n)
            P.dma("sync", ARI, stats[:, 104:108], arS)
            if NC > 1:
                P.wait("gpsimd", arS, arS.n)
                P.inc("gpsimd",
                      lambda e: e.collective_compute("AllReduce", ALU.add, RG,
                                                     [ARI], [ARO]), arS, 16)
            else:
                P.wait("sync", arS, arS.n)
                P.dma("sync", ARO, ARI, arS)
            P.wait("sync", arS, arS.n)
            P.dma("sync", stats[:, 104:108], ARO, arS)
        P.wait("vector", arS, arS.n)
        P.sinc("vector",
               lambda e: e.tensor_scalar(stats[:, 96:100], stats[:, 104:108],
                                         1.0 / NTOT, None, ALU.mult), vs, 1)
        P.sinc("vector",
               lambda e: e.tensor_mul(stats[:, 100:102], stats[:, 96:98],
                                      stats[:, 96:98]), vs, 1)
        P.sinc("vector",
               lambda e: e.tensor_sub(stats[:, 102:104], stats[:, 98:100],
                                      stats[:, 100:102]), vs, 1)
        P.wait("scalar", vs, vs.n)
        P.waitv("scalar", C.ready)
        P.sinc("scalar",
               lambda e: e.activation(stats[:, 108:110], stats[:, 102:104],
                                      AF.Ln, bias=c5b), ss, 1)
        P.sinc("scalar",
               lambda e: e.activation(stats[:, 108:110], stats[:, 108:110],
                                      AF.Exp, scale=-0.5), ss, 1)
        P.wait("vector", ss, ss.n)
        P.sinc("vector",
               lambda e: e.tensor_mul(stats[:, 108:110], stats[:, 108:110],
                                      gam[:, 0:2]), vs, 1)
        bn_rdy = vs.n

        for side, sk in ((0, sk0), (1, sk1)):
            P.waitv("scalar", wv_rdy)
            P.wait("scalar", vs, vs.n)
            P.sinc("scalar",
                   lambda e, sk=sk:
                   e.activation(L.sq[:, 0:128], sk[:], AF.Square,
                                accum_out=stats[:, 110:111]), ss, 1)
            P.sinc("scalar",
                   lambda e: e.activation(stats[:, 111:112], stats[:, 110:111],
                                          AF.Ln, bias=C.cbias(1e-30)), ss, 1)
            P.sinc("scalar",
                   lambda e: e.activation(stats[:, 111:112], stats[:, 111:112],
                                          AF.Exp, scale=-0.5), ss, 1)
            P.wait("vector", ss, ss.n)
            P.sinc("vector",
                   lambda e, sk=sk, side=side:
                   e.tensor_scalar(sknb[:, 128 * side:128 * (side + 1)],
                                   sk[:], stats[:, 111:112], None, ALU.mult),
                   vs, 1)
            j, bank, fa = pool.take2()
            P.waitt("tensor", fa)
            P.wait("tensor", vs, vs.n)
            bb = bank.bitcast(BF16)
            P.inc("tensor",
                  lambda e, bb=bb, side=side:
                  e.transpose(bb[0:128, 0:128],
                              sknb[:, 128 * side:128 * (side + 1)],
                              C.eye[:, :]), mm, 1)
            P.wait("scalar", mm, mm.n)
            P.inc("scalar",
                  lambda e, bb=bb, side=side:
                  e.copy(sknT[:, 128 * side:128 * (side + 1)],
                         bb[0:128, 0:128]), ss, 1)
            pool.consumed(j, ss.n, ss)
        skt_rdy = (ss, ss.n)

        sel_last = 0
        for half in (0, 1):
            wqr = load_wq(sel_last if half else None)
            qproj(half, wqr, True)
            for fc in range(16):
                par = fc % 2
                P.wait("vector", vs, max(vs.n, bn_rdy))
                P.sinc("vector",
                       lambda e, fc=fc, par=par:
                       e.tensor_scalar(q.ch(fc), q.ch(fc),
                                       stats[:, 96 + par:97 + par],
                                       stats[:, 108 + par:109 + par],
                                       ALU.subtract, ALU.mult), vs, 1)
                P.sinc("vector",
                       lambda e, fc=fc, par=par:
                       e.tensor_scalar(q.ch(fc), q.ch(fc),
                                       bet[:, par:par + 1], None, ALU.add),
                       vs, 1)
            for fc in range(16):
                for t5 in range(TH // 512):
                    P.wait("scalar", vs, vs.n)
                    P.wait("scalar", mm, mm.n)
                    P.sinc("scalar",
                           lambda e, fc=fc, t5=t5:
                           e.activation(L.sq[:, 0:512],
                                        q.ch(fc, 512 * t5, 512 * (t5 + 1)),
                                        AF.Square), ss, 1)
                    P.wait("tensor", ss, ss.n)
                    P.inc("tensor",
                          lambda e:
                          e.matmul(p_row[0:1, 0:512], C.ones_b[:, 0:1],
                                   L.sq[:, 0:512], start=True, stop=True),
                          mm, 1)
                    P.wait("scalar", mm, mm.n)
                    P.sinc("scalar",
                           lambda e:
                           e.activation(L.rows[0:1, 0:512], p_row[0:1, 0:512],
                                        AF.Ln,
                                        bias=c30b[0:1, 0:1]), ss, 1)
                    P.sinc("scalar",
                           lambda e:
                           e.activation(L.rows[0:1, 0:512],
                                        L.rows[0:1, 0:512],
                                        AF.Exp, scale=-0.5), ss, 1)
                    P.wait("tensor", ss, ss.n)
                    P.inc("tensor",
                          lambda e:
                          e.matmul(p_bm[0:128, 0:512], C.ones_row[0:1, :],
                                   L.rows[0:1, 0:512], start=True, stop=True),
                          mm, 1)
                    P.wait("vector", mm, mm.n)
                    P.sinc("vector",
                           lambda e, fc=fc, t5=t5:
                           e.tensor_mul(q.ch(fc, 512 * t5, 512 * (t5 + 1)),
                                        q.ch(fc, 512 * t5, 512 * (t5 + 1)),
                                        p_bm[0:128, 0:512]), vs, 1)
            if 0 < dbg < 100 and half == 0:
                dbq = FM(qsp[:, 4096:8192].bitcast(F32), 1, 1).t
                P.wait("vector", vs, vs.n)
                P.sinc("vector",
                       lambda e: e.tensor_copy(dbq[:, 0:112], stats[:, :]),
                       vs, 1)
                P.sinc("vector",
                       lambda e: e.tensor_copy(dbq[:, 1024:2048],
                                               q.ch(dbg - 1)), vs, 1)
                P.wait("sync", vs, vs.n)
                P.dma("sync", DBG, dbq[:, :], st)
                P.wait("sync", st, st.n)
                P.emit()
                return nc
            for t5 in range(TH // 512):
                for side in (0, 1):
                    sXk = (s0k, s1k)[side]
                    for h in range(8):
                        j, bank, fa = pool.take2()
                        P.waitt("tensor", fa)
                        if half == 0 and t5 == 0 and side == 0 and h == 0:
                            P.waitv("tensor", skt_rdy)
                        P.wait("tensor", vs, vs.n)
                        mm_accum(P, mm, bank,
                                 [(sknT[:, 128 * side:128 * (side + 1)],
                                   q.ch(2 * h + side, 512 * t5,
                                        512 * (t5 + 1)), 128)])
                        P.wait("vector", mm, mm.n)
                        P.waitv("vector", const_rdy)
                        P.sinc("vector",
                               lambda e, b=bank:
                               e.tensor_scalar(ksc[:, 0:512], b[0:128, 0:512],
                                               65536.0, MAGIC, ALU.mult,
                                               ALU.add), vs, 1)
                        P.sinc("vector",
                               lambda e:
                               e.tensor_scalar(ksc[:, 0:512], ksc[:, 0:512],
                                               MAGIC, None, ALU.subtract),
                               vs, 1)
                        P.sinc("vector",
                               lambda e:
                               e.scalar_tensor_tensor(
                                   ksc[:, 0:512], ksc[:, 0:512], 128.0,
                                   ncol[:, 0:1].broadcast_to((128, 512)),
                                   ALU.mult, ALU.add), vs, 1)
                        pool.consumed(j, vs.n, vs)
                        for jj in range(4):
                            j2, bank2, fa2 = pool.take2()
                            P.waitt("tensor", fa2)
                            P.wait("tensor", vs, vs.n)
                            P.inc("tensor",
                                  lambda e, b=bank2, jj=jj:
                                  e.transpose(b[0:128, 0:128],
                                              ksc[:, 128 * jj:128 * (jj + 1)],
                                              C.eye_f[:, :]), mm, 1)
                            P.wait("scalar", mm, mm.n)
                            P.inc("scalar",
                                  lambda e, b=bank2, sXk=sXk, jj=jj, h=h:
                                  e.copy(sXk[:, 1024 * jj + 128 * h:
                                              1024 * jj + 128 * (h + 1)],
                                         b[0:128, 0:128]), ss, 1)
                            pool.consumed(j2, ss.n, ss)
                if dbg >= 100 and dbg - 100 == half * (TH // 512) + t5:
                    dbq = FM(qsp[:, 4096:8192].bitcast(F32), 1, 1).t
                    P.wait("vector", ss, ss.n)
                    P.sinc("vector",
                           lambda e: e.tensor_copy(dbq[:, 0:1024],
                                                   s0k[:, 0:1024]), vs, 1)
                    P.sinc("vector",
                           lambda e: e.tensor_copy(dbq[:, 1024:2048],
                                                   s1k[:, 0:1024]), vs, 1)
                    P.wait("sync", vs, vs.n)
                    P.dma("sync", DBG, dbq[:, :], st)
                    P.wait("sync", st, st.n)
                    P.emit()
                    return nc
                P.wait("vector", ss, ss.n)
                for jj in range(4):
                    tt = half * (NT // 2) + t5 * 4 + jj
                    for h in range(8):
                        s0s = s0k[:, 1024 * jj + 128 * h:
                                  1024 * jj + 128 * (h + 1)]
                        s1s = s1k[:, 1024 * jj + 128 * h:
                                  1024 * jj + 128 * (h + 1)]
                        P.sinc("vector",
                               lambda e, s0s=s0s: e.max(selb[:, 0:8], s0s),
                               vs, 1)
                        P.sinc("vector",
                               lambda e, s0s=s0s:
                               e.match_replace(ksc[:, 0:128], selb[:, 0:8],
                                               s0s, -3e38), vs, 1)
                        P.sinc("vector",
                               lambda e: e.max(selb[:, 8:16], ksc[:, 0:128]),
                               vs, 1)
                        P.sinc("vector",
                               lambda e, s1s=s1s: e.max(selb[:, 16:24], s1s),
                               vs, 1)
                        P.sinc("vector",
                               lambda e, s1s=s1s:
                               e.match_replace(ksc[:, 0:128], selb[:, 16:24],
                                               s1s, -3e38), vs, 1)
                        P.sinc("vector",
                               lambda e: e.max(selb[:, 24:32], ksc[:, 0:128]),
                               vs, 1)
                        P.sinc("vector",
                               lambda e:
                               e.tensor_copy(selb[:, 32:64].bitcast(I32),
                                             selb[:, 0:32]), vs, 1)
                        P.sinc("vector",
                               lambda e:
                               e.tensor_scalar(selb[:, 32:64].bitcast(I32),
                                               selb[:, 32:64].bitcast(I32),
                                               127, None, ALU.bitwise_and),
                               vs, 1)
                        P.sinc("vector",
                               lambda e:
                               e.tensor_copy(selb[:, 64:96],
                                             selb[:, 32:64].bitcast(I32)),
                               vs, 1)
                        P.sinc("vector",
                               lambda e:
                               e.tensor_sub(selb[:, 96:128], selb[:, 0:32],
                                            selb[:, 64:96]), vs, 1)
                        P.sinc("vector",
                               lambda e:
                               e.tensor_scalar(selb[:, 96:128],
                                               selb[:, 96:128],
                                               1.0 / 128, None, ALU.mult),
                               vs, 1)
                        P.sinc("vector",
                               lambda e:
                               e.tensor_tensor(
                                   selb[:, 256:512]
                                   .rearrange("p (r c) -> p r c", c=16),
                                   selb[:, 96:112]
                                   .rearrange("p (r c) -> p r c", c=1)
                                   .broadcast_to((128, 16, 16)),
                                   selb[:, 112:128]
                                   .rearrange("p (r c) -> p r c", r=1)
                                   .broadcast_to((128, 16, 16)),
                                   ALU.add), vs, 1)
                        P.sinc("vector",
                               lambda e:
                               e.tensor_scalar(selb[:, 512:768],
                                               selb[:, 256:512],
                                               1.0 / 256, MAGIC, ALU.mult,
                                               ALU.add), vs, 1)
                        P.sinc("vector",
                               lambda e:
                               e.tensor_scalar(selb[:, 512:768],
                                               selb[:, 512:768],
                                               MAGIC, None, ALU.subtract),
                               vs, 1)
                        P.sinc("vector",
                               lambda e:
                               e.scalar_tensor_tensor(
                                   selb[:, 256:512]
                                   .rearrange("p (r c) -> p r c", c=16),
                                   selb[:, 64:80]
                                   .rearrange("p (r c) -> p r c", c=1)
                                   .broadcast_to((128, 16, 16)), 128.0,
                                   selb[:, 80:96]
                                   .rearrange("p (r c) -> p r c", r=1)
                                   .broadcast_to((128, 16, 16)),
                                   ALU.mult, ALU.add), vs, 1)
                        P.sinc("vector",
                               lambda e:
                               e.scalar_tensor_tensor(selb[:, 512:768],
                                                      selb[:, 512:768],
                                                      16384.0,
                                                      selb[:, 256:512],
                                                      ALU.mult, ALU.add),
                               vs, 1)
                        P.sinc("vector",
                               lambda e: e.max(selb[:, 0:8],
                                               selb[:, 512:768]), vs, 1)
                        P.sinc("vector",
                               lambda e:
                               e.match_replace(ksc[:, 0:256], selb[:, 0:8],
                                               selb[:, 512:768], -3e38),
                               vs, 1)
                        P.sinc("vector",
                               lambda e: e.max(selb[:, 8:16], ksc[:, 0:256]),
                               vs, 1)
                        P.sinc("vector",
                               lambda e:
                               e.tensor_copy(selb[:, 48:64].bitcast(I32),
                                             selb[:, 0:16]), vs, 1)
                        P.sinc("vector",
                               lambda e:
                               e.tensor_scalar(selb[:, 48:64].bitcast(I32),
                                               selb[:, 48:64].bitcast(I32),
                                               16383, None, ALU.bitwise_and),
                               vs, 1)
                        P.sinc("vector",
                               lambda e:
                               e.tensor_copy(selb[:, 16:32],
                                             selb[:, 48:64].bitcast(I32)),
                               vs, 1)
                        P.sinc("vector",
                               lambda e:
                               e.tensor_sub(selb[:, 32:48], selb[:, 0:16],
                                            selb[:, 16:32]), vs, 1)
                        P.sinc("vector",
                               lambda e, tt=tt, h=h:
                               e.tensor_copy(e_all[:, 128 * tt + 16 * h:
                                                   128 * tt + 16 * (h + 1)],
                                             selb[:, 16:32]), vs, 1)
                        P.sinc("vector",
                               lambda e: e.max(m8s[:, 0:8], selb[:, 32:48]),
                               vs, 1)
                        P.sinc("vector",
                               lambda e:
                               e.tensor_scalar(lcc[:, 0:1], m8s[:, 0:1],
                                               -(2.0 ** -22), None, ALU.mult),
                               vs, 1)
                        P.wait("scalar", vs, vs.n)
                        P.sinc("scalar",
                               lambda e:
                               e.activation(exw[:, 0:16], selb[:, 32:48],
                                            AF.Exp, scale=2.0 ** -22,
                                            bias=lcc[:, 0:1],
                                            accum_out=lcc[:, 1:2]), ss, 1)
                        P.wait("vector", ss, ss.n)
                        P.sinc("vector",
                               lambda e:
                               e.reciprocal(lcc[:, 0:1], lcc[:, 1:2]), vs, 1)
                        P.sinc("vector",
                               lambda e, tt=tt, h=h:
                               e.tensor_scalar(w_all[:, 128 * tt + 16 * h:
                                                     128 * tt + 16 * (h + 1)],
                                               exw[:, 0:16], lcc[:, 0:1],
                                               None, ALU.mult), vs, 1)
                sel_last = vs.n
        sel_rdy = (vs, vs.n)

        P.waitv("sync", cat_rdy)
        P.waitv("gpsimd", cat_rdy)
        slot_free = [None] * 4
        shuf_done = {}
        icol_used = 0
        outtm_evac = 0
        zc_mm = [0, 0]
        gel_ss = [0, 0]
        z_vs = [0, 0]
        xr_mm = {}
        for tt in range(NT):
            P.waitv("scalar", sel_rdy)
            if tt > 0:
                for g4 in gat4:
                    P.wait("scalar", g4, g4.n)
                P.wait("scalar", vs, icol_used)
            for kq in range(2):
                for sq in range(2):
                    P.sinc("scalar",
                           lambda e, kq=kq, sq=sq, tt=tt:
                           e.copy(ECOLf[64 * kq:64 * (kq + 1),
                                        64 * sq:64 * (sq + 1)],
                                  e_all[64 * sq:64 * (sq + 1),
                                        128 * tt + kq:128 * (tt + 1):2]),
                           ss, 1)
                    P.sinc("scalar",
                           lambda e, kq=kq, sq=sq, tt=tt:
                           e.copy(WCOL[64 * kq:64 * (kq + 1),
                                       64 * sq:64 * (sq + 1)],
                                  w_all[64 * sq:64 * (sq + 1),
                                        128 * tt + kq:128 * (tt + 1):2]),
                           ss, 1)
            P.wait("vector", ss, ss.n)
            if tt > 0:
                for g4 in gat4:
                    P.wait("vector", g4, g4.n)
            P.sinc("vector",
                   lambda e:
                   e.tensor_copy(ICOL32[:, 0:128], ECOLf[:, 0:128]), vs, 1)
            icol_rdy = vs.n
            for slab in range(2):
                xsl = xnslab2[slab % 2]
                for k in range(8):
                    j, bank, fa = pool.take2()
                    P.waitt("tensor", fa)
                    bb = bank.bitcast(BF16)
                    P.inc("tensor",
                          lambda e, bb=bb, k=k, tt=tt, slab=slab:
                          e.transpose(bb[0:64, 0:128],
                                      xn.chp(k, 0, 128,
                                             128 * tt + 64 * slab,
                                             128 * tt + 64 * (slab + 1)),
                                      C.eye[:, :]), mm, 1)
                    P.wait("scalar", mm, mm.n)
                    key = (tt * 2 + slab - 2)
                    if key >= 0 and key in xr_mm:
                        P.wait("scalar", mm, xr_mm[key])
                    P.inc("scalar",
                          lambda e, bb=bb, xsl=xsl, k=k:
                          e.copy(xsl[0:64, 128 * k:128 * (k + 1)],
                                 bb[0:64, 0:128]), ss, 1)
                    pool.consumed(j, ss.n, ss)
                xr = XR2[slab % 2]
                for nh in range(2):
                    j, bank, fa = pool.take2()
                    P.waitt("tensor", fa)
                    P.wait("tensor", ss, ss.n)
                    if tt == 0 and slab == 0 and nh == 0:
                        P.waitv("tensor", const_rdy)
                    P.inc("tensor",
                          lambda e, b=bank, xsl=xsl, nh=nh:
                          e.matmul(b[0:128, 0:512], REPM[0:64, :],
                                   xsl[0:64, 512 * nh:512 * (nh + 1)],
                                   start=True, stop=True), mm, 1)
                    if nh == 1:
                        xr_mm[tt * 2 + slab] = mm.n
                    P.wait("vector", mm, mm.n)
                    P.sinc("vector",
                           lambda e, b=bank, xr=xr, nh=nh:
                           e.tensor_copy(xr[:, 512 * nh:512 * (nh + 1)],
                                         b[0:128, 0:512]), vs, 1)
                    pool.consumed(j, vs.n, vs)
                for qd in range(64):
                    cix = slab * 64 + qd
                    gcall = tt * 128 + cix
                    slot = gcall % 4
                    dst = dst4[slot]
                    if slot_free[slot] is not None:
                        vv, mv = slot_free[slot]
                        P.wait("gpsimd", vs, vv)
                        P.wait("gpsimd", mm, mv)
                    P.wait("gpsimd", vs, icol_rdy)
                    gsem = gat4[slot]
                    P.on("gpsimd",
                         lambda e, dst=dst, cix=cix, gsem=gsem:
                         e.indirect_dma_start(
                             out=dst[:, 0:2048], in_=CAT, out_offset=None,
                             in_offset=IndirectOffsetOnAxis(
                                 ICOL32[:, cix:cix + 1], 0)).then_inc(gsem.h,
                                                                      16))
                    gsem.n += 16
                    par = gcall % 2
                    zcol = zcol2[par]
                    P.wait("vector", gsem, gsem.n)
                    P.wait("vector", ss, gel_ss[par])
                    P.sinc("vector",
                           lambda e, dst=dst, xr=xr, par=par:
                           e.scalar_tensor_tensor(dscr[:, 0:1024],
                                                  dst[:, 0:1024], 1.0, xr,
                                                  ALU.mult, ALU.mult,
                                                  accum_out=eic2[:, par:
                                                                 par + 1]),
                           vs, 1)
                    P.wait("scalar", vs, vs.n)
                    P.wait("scalar", vs, z_vs[par])
                    P.sinc("scalar",
                           lambda e, par=par:
                           e.activation(gec2[:, par:par + 1],
                                        eic2[:, par:par + 1], AF.Gelu),
                           ss, 1)
                    gel_ss[par] = ss.n
                    P.wait("vector", ss, ss.n)
                    P.wait("vector", mm, zc_mm[par])
                    P.sinc("vector",
                           lambda e, cix=cix, zcol=zcol, par=par:
                           e.tensor_scalar(zcol[:, 0:64], BANDM[:],
                                           WCOL[:, cix:cix + 1],
                                           gec2[:, par:par + 1],
                                           ALU.mult, ALU.mult), vs, 1)
                    z_vs[par] = vs.n
                    zv = vs.n
                    P.wait("tensor", vs, zv)
                    if cix == 0 and tt > 0:
                        P.wait("tensor", vs, outtm_evac)
                    P.on("tensor",
                         lambda e, dst=dst, slab=slab, qd=qd, zcol=zcol:
                         e.matmul(pc0[64 * slab:64 * (slab + 1), 0:512],
                                  zcol[:, 0:64], dst[:, 1024:1536],
                                  start=(qd == 0), stop=(qd == 63)))
                    P.inc("tensor",
                          lambda e, dst=dst, slab=slab, qd=qd, zcol=zcol:
                          e.matmul(pc1[64 * slab:64 * (slab + 1), 0:512],
                                   zcol[:, 0:64], dst[:, 1536:2048],
                                   start=(qd == 0), stop=(qd == 63)), mm, 1)
                    slot_free[slot] = (vs.n, mm.n)
                    zc_mm[par] = mm.n
            icol_used = vs.n
            P.wait("vector", mm, mm.n)
            P.sinc("vector",
                   lambda e: e.tensor_copy(outtm[:, 0:512], pc0[0:128, :]),
                   vs, 1)
            P.sinc("vector",
                   lambda e: e.tensor_copy(outtm[:, 512:1024], pc1[0:128, :]),
                   vs, 1)
            outtm_evac = vs.n
            for k in range(8):
                j, bank, fa = pool.take2()
                P.waitt("tensor", fa)
                P.wait("tensor", vs, outtm_evac)
                P.inc("tensor",
                      lambda e, b=bank, k=k:
                      e.transpose(b[0:128, 0:128],
                                  outtm[:, 128 * k:128 * (k + 1)],
                                  C.eye_f[:, :]), mm, 1)
                P.wait("vector", mm, mm.n)
                P.sinc("vector",
                       lambda e, b=bank, k=k, tt=tt:
                       e.tensor_add(x.chp(k, 0, 128, 128 * tt, 128 * (tt + 1)),
                                    x.chp(k, 0, 128, 128 * tt, 128 * (tt + 1)),
                                    b[0:128, 0:128]), vs, 1)
                pool.consumed(j, vs.n, vs)
        store_x(P, x, XOUT, T, st, (vs, vs.n))
        P.emit()
    return nc


from concourse.bass_utils import run_bass_kernel_spmd

_NC = 8
_T = 2048

_cache = {}


def _launch(key, builder, ims):
    if key not in _cache:
        _cache[key] = builder()
    nc = _cache[key]
    return run_bass_kernel_spmd(nc, ims, core_ids=list(range(_NC))).results


def kernel(**inputs):
    x = inputs["L0_attn_x"]
    wsel = lambda pfx: {k: np.ascontiguousarray(v)
                        for k, v in inputs.items()
                        if k.startswith(pfx) and k in SHAPES}

    cur = [np.ascontiguousarray(x[c].T) for c in range(_NC)]

    for key, builder, pfx in (
            ("l0a", lambda: build_l0attn_launch(_T), "L0_attn"),
            ("l0f", lambda: build_l0ffn_launch(_T), "L0_ffn"),
            ("l1a", lambda: build_l1attn_launch(_T), "L1_attn"),
            ("l1f", lambda: build_l1ffn_launch(_T), "L1_ffn"),
            ("l2a", lambda: build_l2attn_launch(_T), "L2_attn"),
            ("l2f", lambda: build_l2ffn_launch(_T), "L2_ffn")):
        w = wsel(pfx)
        ims = [dict(XIN=cur[c], **w) for c in range(_NC)]
        res = _launch(key, builder, ims)
        cur = [np.ascontiguousarray(res[c]["XOUT"]) for c in range(_NC)]

    res = _launch("ln3a", lambda: build_ln3a_launch(_T),
                  [dict(XIN=cur[c]) for c in range(_NC)])
    xnt = [res[c]["XNT"] for c in range(_NC)]

    PP = _T // _NC
    w3a = wsel("L3_attn")
    w3a = {k: w3a[k] for k in ("L3_attn_in_proj_weight", "L3_attn_out_weight")}
    ims = [dict(XNT=np.ascontiguousarray(
        np.concatenate([xnt[b][PP * c:PP * (c + 1)] for b in range(_NC)],
                       axis=0)), **w3a) for c in range(_NC)]
    res = _launch("l3a", lambda: build_l3attn_launch(_T, _NC), ims)
    a3 = [res[c]["A3T"] for c in range(_NC)]

    w3f = wsel("L3_ffn")
    a3c = [np.ascontiguousarray(
        np.concatenate([a3[r][PP * c:PP * (c + 1)] for r in range(_NC)],
                       axis=0)) for c in range(_NC)]
    ims = [dict(XIN=cur[c], A3T=a3c[c], **w3f) for c in range(_NC)]
    res = _launch("l3fs",
                  lambda: build_l3ffn_launch(_T, _NC, stats_only=True), ims)
    stats8 = np.ascontiguousarray(
        np.concatenate([res[c]["STATSOUT"] for c in range(_NC)], axis=0))
    ims = [dict(XIN=cur[c], A3T=a3c[c], STATS8=stats8, **w3f)
           for c in range(_NC)]
    res = _launch("l3f",
                  lambda: build_l3ffn_launch(_T, _NC, ext_stats=True), ims)
    out = np.stack([np.ascontiguousarray(res[c]["XOUT"]).T
                    for c in range(_NC)], axis=0)
    return out.astype(np.float32)
